# version 22
# baseline (speedup 1.0000x reference)
import numpy as np
from contextlib import ExitStack

import concourse.bass as bass
import concourse.mybir as mybir
from concourse.bass_utils import run_bass_kernel_spmd

AF = mybir.ActivationFunctionType
ALU = mybir.AluOpType
AX = mybir.AxisListType.X
F32 = mybir.dt.float32
BF16 = mybir.dt.bfloat16

T = 2048
C = 1024
FF = 4096
NE = 8
CT = C // 128
TT = T // 128
NTC = T // 512
LN_EPS = 1e-5
NSLAB = 4

I_ONE, I_NEG1, I_EPS, I_E6, I_E8 = 0, 1, 2, 3, 4


class G:

    def __init__(self, nc, block):
        self.nc = nc
        self.block = block
        self.sem = {}
        self.c = {}
        self._cms = []
        for name in ("gp", "sy", "pe", "ve", "se", "pc"):
            cm = nc.semaphore("sem_" + name)
            self._cms.append(cm)
            self.sem[name] = cm.__enter__()
            self.c[name] = 0

    def i(self, inst, s, n=1):
        inst.then_inc(self.sem[s], n)
        self.c[s] += n
        return self.c[s]

    def dma(self, eng, q, out, in_, accum=None, transpose=False):
        eng.wait_ge(self.sem[q], self.c[q])
        if transpose:
            inst = eng.dma_start_transpose(out, in_)
        elif accum is not None:
            inst = eng.dma_start(out=out, in_=in_, accum_op=accum)
        else:
            inst = eng.dma_start(out=out, in_=in_)
        return self.i(inst, q, 16)

    def w(self, eng, s, v):
        if v and v > 0:
            eng.wait_ge(self.sem[s], v)

    def barrier(self):
        for name, sec in (("pe", self.block.tensor), ("ve", self.block.vector),
                          ("se", self.block.scalar)):
            @sec
            def _(e, name=name):
                e.drain()
                self.i(e.nop(), name)
        tgt = dict(self.c)
        for sec in (self.block.tensor, self.block.vector, self.block.scalar,
                    self.block.sync, self.block.gpsimd):
            @sec
            def _(e):
                for s in ("gp", "sy", "pe", "ve", "se", "pc"):
                    self.w(e, s, tgt[s])


def _r3(ap, a, inner):
    return ap.rearrange("p (a n) -> p a n", a=a)



def build_program(upto=None):
    nc = bass.Bass("TRN2", target_bir_lowering=False)
    dtn = nc.dram_tensor
    E = {}

    E["upto"] = upto
    for nm, shp in (("x", [T, C]), ("xT", [C, T]), ("WvT", [C, C]),
                    ("WrT", [C, C]), ("WoT", [C, C]), ("gw", [C, NE]),
                    ("w1", [NE, C, FF]), ("w3", [NE, C, FF]), ("w2", [NE, FF, C]),
                    ("Wr1", [C, C]), ("Ww1", [C, C]), ("Wk1", [C, C]),
                    ("Wv1", [C, C]), ("Wo1", [C, C]),
                    ("Wky", [C, FF]), ("Wvl", [FF, C]), ("Wrc", [C, C]),
                    ("rw", [C, NE]), ("rb", [1, NE]), ("ew", [NE, C, C]),
                    ("ebv", [NE, C]), ("smal", [128, 64]), ("idn", [128, 128])):
        E[nm + "_h"] = dtn(nm, shp, F32, kind="ExternalInput").ap()

    for nm in ("m0r", "r0r", "m1r", "r1r", "m2r", "r2r", "m3r", "r3r", "p4r", "q4r"):
        E[nm + "_h"] = dtn(nm, [1, T], F32, kind="Internal").ap()
    for nm in ("acc0", "x1", "acc1", "xx"):
        E[nm + "_h"] = dtn(nm, [T, C], F32, kind="Internal").ap()
    E["x1b_h"] = dtn("x1b", [T, C], BF16, kind="Internal").ap()
    E["x1r_h"] = dtn("x1r", [T, C], BF16, kind="Internal").ap()
    E["xxb_h"] = dtn("xxb", [T, C], BF16, kind="Internal").ap()
    E["out_h"] = dtn("out", [T, C], F32, kind="ExternalOutput").ap()

    E["dbg"] = {}
    if upto is not None:
        for nm, shp, dt_ in (("xnT", [CT, 128, T], BF16), ("wi", [128, TT * NE], F32),
                             ("acc0", [T, C], F32), ("rwkv", [CT, 128, T], BF16),
                             ("x1", [T, C], F32), ("xnT1", [CT, 128, T], BF16),
                             ("acc1", [T, C], F32), ("rwkv1", [CT, 128, T], BF16),
                             ("xx", [T, C], F32), ("mw", [128, TT * NE], F32)):
            E["dbg"][nm] = dtn("dbg_" + nm, shp, dt_, kind="ExternalOutput").ap()

    al = nc.alloc_sbuf_tensor
    E["smal"] = al("smalS", [128, 64], F32)
    sm = E["smal"]
    E["cst"] = sm[:, 0:8]
    for i, nm in enumerate(("td0c", "tf0c", "td1c", "g1c", "b1c", "g2c", "b2c")):
        E[nm] = sm[:, 8 * (i + 1):8 * (i + 2)]
    E["idn"] = al("idnS", [128, 128], F32)
    E["gwb"] = al("gwb", [128, CT * NE], BF16)
    E["rwb"] = al("rwb", [128, CT * NE], BF16)
    E["ebb"] = al("ebb", [NE, C], BF16)
    E["rbbc"] = al("rbbc", [128, NE], F32)
    E["wiA"] = al("wiA", [128, TT * NE], F32)
    E["mwA"] = al("mwA", [128, TT * NE], F32)
    E["m1c"] = al("m1c", [128, TT], F32)
    E["r1c"] = al("r1c", [128, TT], F32)
    E["csc"] = al("cscS", [128, 32], F32)
    E["sc2"] = al("sc2S", [128, TT], F32)
    E["sc3"] = al("sc3S", [128, TT], F32)
    E["trs"] = al("trsS", [16, 128], F32)

    E["pb"] = [nc.alloc_psum_tensor(f"pb{i}", [128, 512], F32) for i in range(8)]

    with nc.Block() as block:
        g = G(nc, block)
        _emit(g, E)

    return nc


def _emit(g, E):
    nc = g.nc
    upto = E["upto"]

    _p0_preload(g, E)
    g.barrier()

    l0st = ExitStack()
    E["xnT"] = [l0st.enter_context(nc.sbuf_tensor(f"xnT{i}", [128, T], BF16))
                for i in range(CT)]
    _p12_ln0(g, E)
    g.barrier()
    if upto == "P2":
        _dump_fm(g, E, "xnT", E["xnT"]); _finish(g, E); return

    _p3_gate(g, E)
    g.barrier()
    E["xn32_st"].close()
    if upto == "P3":
        _dump_plain(g, E, "wi", E["wiA"][:]); _finish(g, E); return

    _p4_moe_l0(g, E)
    g.barrier()
    if upto == "P4":
        _dump_hbm(g, E, "acc0", E["acc0_h"]); _finish(g, E); return

    E["rwkv"] = [l0st.enter_context(nc.sbuf_tensor(f"rwkv{i}", [128, T], BF16))
                 for i in range(CT)]
    _p5_attn_l0(g, E)
    g.barrier()
    if upto == "P5":
        _dump_fm(g, E, "rwkv", E["rwkv"]); _finish(g, E); return

    _p6_combine_l0(g, E)
    g.barrier()
    if upto == "P6":
        _dump_hbm(g, E, "x1", E["x1_h"]); _finish(g, E); return
    l0st.close()

    q3st = ExitStack()
    E["xn1T"] = [q3st.enter_context(nc.sbuf_tensor(f"xn1Tq3_{i}", [128, T], BF16))
                 for i in range(CT)]
    l1st = ExitStack()
    E["xnT1"] = [l1st.enter_context(nc.sbuf_tensor(f"xnT1_{i}", [128, T], BF16))
                 for i in range(CT)]
    _q12_l1(g, E, l1st, upto)
    g.barrier()
    if upto == "Q1":
        _dump_fm(g, E, "xnT1", E["xnT1"]); _finish(g, E); return
    if upto == "Q2":
        _dump_hbm(g, E, "acc1", E["acc1_h"])
        _dump_plain(g, E, "mw", E["mwA"][:])
        _finish(g, E); return
    l1st.close()

    l1bst = ExitStack()
    E["rwkv1"] = [l1bst.enter_context(nc.sbuf_tensor(f"rwkv1_{i}", [128, T], BF16))
                  for i in range(CT)]
    _q3_timemix_l1(g, E)
    g.barrier()
    if upto == "Q3":
        _dump_fm(g, E, "rwkv1", E["rwkv1"]); _finish(g, E); return


    _q4a_xx(g, E)
    g.barrier()
    l1bst.close()
    q3st.close()
    if upto == "Q4A":
        _dump_hbm(g, E, "xx", E["xx_h"]); _finish(g, E); return

    _q4b_final(g, E)
    _finish(g, E)



def _dump_fm(g, E, key, tiles):
    out = E["dbg"][key]

    @g.block.sync
    def _(sy):
        for ct, tl in enumerate(tiles):
            g.dma(sy, "sy", out[ct], tl[:])


def _dump_plain(g, E, key, ap):
    out = E["dbg"][key]

    @g.block.sync
    def _(sy):
        g.dma(sy, "sy", out, ap)


def _dump_hbm(g, E, key, hbm_ap):
    out = E["dbg"][key]

    @g.block.sync
    def _(sy):
        g.dma(sy, "sy", out, hbm_ap)


def _finish(g, E):
    @g.block.sync
    def _(sy):
        g.w(sy, "gp", g.c["gp"])
        g.w(sy, "sy", g.c["sy"])



def _p0_preload(g, E):
    @g.block.sync
    def _(sy):
        g.dma(sy, "sy", E["smal"][:], E["smal_h"])
        g.dma(sy, "sy", E["idn"][:], E["idn_h"])
        g.dma(sy, "sy", E["rbbc"][:], E["rb_h"].broadcast_to([128, NE]))

    @g.block.gpsimd
    def _(gp):
        g.dma(gp, "gp", _r3(E["gwb"][:], CT, NE),
              E["gw_h"].rearrange("(a p) e -> p a e", p=128))
        g.dma(gp, "gp", _r3(E["rwb"][:], CT, NE),
              E["rw_h"].rearrange("(a p) e -> p a e", p=128))
        g.dma(gp, "gp", E["ebb"][:], E["ebv_h"])



def _p12_ln0(g, E):
    nc = g.nc
    pb = E["pb"]
    cst = E["cst"]
    ones = cst[:, I_ONE:I_ONE + 1]
    neg1 = cst[0:1, I_NEG1:I_NEG1 + 1]
    st = ExitStack()
    sb = nc.sbuf_tensor
    E["xn32_st"] = ExitStack()
    xTs = [E["xn32_st"].enter_context(sb(f"xT{i}", [128, T], F32)) for i in range(CT)]
    E["xn32"] = xTs
    xq = st.enter_context(sb("xqA", [128, T], F32))
    row0 = st.enter_context(sb("row0A", [1, T], F32))
    row1 = st.enter_context(sb("row1A", [1, T], F32))
    row2 = st.enter_context(sb("row2A", [1, T], F32))
    m0bc = st.enter_context(sb("m0bcA", [128, T], F32))
    r0bc = st.enter_context(sb("r0bcA", [128, T], F32))
    xnT = E["xnT"]

    ld_ct = [0] * CT
    ld_last = [0]

    @g.block.sync
    def _(sy):
        for ct in range(CT):
            ld_ct[ct] = g.dma(sy, "sy", xTs[ct][:],
                              E["xT_h"][ct * 128:(ct + 1) * 128, :])
        ld_last[0] = ld_ct[CT - 1]

    summ = {}

    @g.block.tensor
    def _(pe):
        for ct in range(CT):
            g.w(pe, "sy", ld_ct[ct])
            for tc in range(NTC):
                mm = pe.matmul(pb[tc][0:1, :], ones,
                               xTs[ct][:, tc * 512:(tc + 1) * 512],
                               start=(ct == 0), stop=(ct == CT - 1))
        summ[0] = g.i(mm, "pe")

    sqd = {}
    sqm = {}
    for ct in range(CT):
        @g.block.scalar
        def _(se, ct=ct):
            g.w(se, "sy", ld_ct[ct])
            if ct >= 1:
                g.w(se, "pe", sqm[ct - 1])
            sqd[ct] = g.i(se.activation(xq[:], xTs[ct][:], AF.Square), "se")

        @g.block.tensor
        def _(pe, ct=ct):
            g.w(pe, "se", sqd[ct])
            for tc in range(NTC):
                mm = pe.matmul(pb[4 + tc][0:1, :], ones,
                               xq[:, tc * 512:(tc + 1) * 512],
                               start=(ct == 0), stop=(ct == CT - 1))
            sqm[ct] = g.i(mm, "pe")

    rows_done = [0]
    row0_done = [0]

    @g.block.vector
    def _(ve):
        g.w(ve, "pe", sqm[CT - 1])
        g.w(ve, "pe", summ[0])
        for tc in range(NTC):
            ve.tensor_copy(row0[0:1, tc * 512:(tc + 1) * 512], pb[tc][0:1, :])
            ve.tensor_copy(row1[0:1, tc * 512:(tc + 1) * 512], pb[4 + tc][0:1, :])
        row0_done[0] = g.i(ve.tensor_scalar_mul(row0[:], row0[:], 1.0 / C), "ve")
        ve.tensor_scalar_mul(row1[:], row1[:], 1.0 / C)
        ve.drain()
        ve.tensor_mul(row2[:], row0[:], row0[:])
        ve.drain()
        rows_done[0] = g.i(ve.tensor_sub(row1[:], row1[:], row2[:]), "ve")

    sd_done = [0]

    @g.block.scalar
    def _(se):
        g.w(se, "ve", rows_done[0])
        se.activation(row1[:], row1[:], AF.Sqrt, bias=cst[0:1, I_EPS:I_EPS + 1])
        se.drain()
        se.activation(row2[:], row1[:], AF.Ln)
        se.drain()
        sd_done[0] = g.i(se.activation(row1[:], row2[:], AF.Exp, scale=neg1), "se")

    bc_done = [0]

    @g.block.sync
    def _(sy):
        g.w(sy, "se", sd_done[0])
        g.w(sy, "ve", rows_done[0])
        g.w(sy, "ve", row0_done[0])
        g.dma(sy, "sy", E["m0r_h"], row0[:])
        g.dma(sy, "sy", E["r0r_h"], row1[:])
        g.dma(sy, "sy", m0bc[:], E["m0r_h"].broadcast_to([128, T]))
        bc_done[0] = g.dma(sy, "sy", r0bc[:], E["r0r_h"].broadcast_to([128, T]))

    @g.block.vector
    def _(ve):
        g.w(ve, "sy", bc_done[0])
        for ct in range(CT):
            ve.tensor_sub(xTs[ct][:], xTs[ct][:], m0bc[:])
            ve.tensor_mul(xTs[ct][:], xTs[ct][:], r0bc[:])
            g.i(ve.tensor_copy(xnT[ct][:], xTs[ct][:]), "ve")

    g.barrier()
    st.close()



def _p3_gate(g, E):
    nc = g.nc
    pb = E["pb"]
    xnT = E["xnT"]
    wiA = E["wiA"]
    csc = E["csc"]
    st = ExitStack()
    lgt = st.enter_context(nc.sbuf_tensor("lgtP3", [128, 2 * NE], F32))
    l2t = st.enter_context(nc.sbuf_tensor("l2tP3", [128, 2 * NE], F32))
    gwf = st.enter_context(nc.sbuf_tensor("gwfP3", [128, CT * NE], F32))
    xn32 = E["xn32"]

    gld = [0]

    @g.block.sync
    def _(sy):
        gld[0] = g.dma(sy, "sy", _r3(gwf[:], CT, NE),
                       E["gw_h"].rearrange("(a p) e -> p a e", p=128))

    lgd = {}
    fin = {}
    mid = {}
    for tt in range(TT):
        bank = 6 + (tt % 2)

        @g.block.tensor
        def _(pe, tt=tt, bank=bank):
            if tt >= 2:
                g.w(pe, "ve", fin[tt - 2])
            g.w(pe, "sy", gld[0])
            for ct in range(CT):
                mm = pe.matmul(pb[bank][:, 0:NE],
                               xn32[ct][:, tt * 128:(tt + 1) * 128],
                               gwf[:, ct * NE:(ct + 1) * NE],
                               start=(ct == 0), stop=(ct == CT - 1))
            lgd[tt] = g.i(mm, "pe")

        @g.block.vector
        def _(ve, tt=tt, bank=bank):
            sl = slice((tt % 2) * NE, (tt % 2) * NE + NE)
            lg, l2 = lgt[:, sl], l2t[:, sl]
            g.w(ve, "pe", lgd[tt])
            ve.tensor_copy(lg, pb[bank][:, 0:NE])
            ve.drain()
            ve.reduce_max(csc[:, 0:1], lg, axis=AX)
            ve.drain()
            ve.tensor_single_scalar(l2, lg, csc[:, 0:1], ALU.is_lt)
            ve.drain()
            ve.tensor_scalar(l2, l2, 1.0, 1e30, ALU.subtract, ALU.mult)
            ve.drain()
            ve.tensor_add(l2, l2, lg)
            ve.drain()
            ve.reduce_max(csc[:, 1:2], l2, axis=AX)
            ve.drain()
            mid[tt] = g.i(ve.tensor_sub(csc[:, 2:3], csc[:, 1:2], csc[:, 0:1]), "ve")

        @g.block.scalar
        def _(se, tt=tt):
            g.w(se, "ve", mid[tt])
            mid[("e", tt)] = g.i(se.activation(csc[:, 3:4], csc[:, 2:3], AF.Exp), "se")

        @g.block.vector
        def _(ve, tt=tt):
            sl = slice((tt % 2) * NE, (tt % 2) * NE + NE)
            lg, l2 = lgt[:, sl], l2t[:, sl]
            g.w(ve, "se", mid[("e", tt)])
            ve.tensor_scalar_add(csc[:, 4:5], csc[:, 3:4], 1.0)
            ve.drain()
            ve.reciprocal(csc[:, 5:6], csc[:, 4:5])
            ve.drain()
            ve.tensor_mul(csc[:, 6:7], csc[:, 3:4], csc[:, 5:6])
            ve.drain()
            wsl = wiA[:, tt * NE:(tt + 1) * NE]
            ve.tensor_scalar(wsl, lg, csc[:, 0:1], csc[:, 5:6], ALU.is_equal, ALU.mult)
            ve.drain()
            ve.tensor_scalar(l2, l2, csc[:, 1:2], csc[:, 6:7], ALU.is_equal, ALU.mult)
            ve.drain()
            fin[tt] = g.i(ve.tensor_add(wsl, wsl, l2), "ve")

    g.barrier()
    st.close()



def _p4_moe_l0(g, E):
    nc = g.nc
    pb = E["pb"]
    xnT = E["xnT"]
    wiA = E["wiA"]
    st = ExitStack()
    sb = nc.sbuf_tensor
    h = [st.enter_context(sb(f"hP4_{i}", [128, T], BF16)) for i in range(8)]
    w1s = [st.enter_context(sb(f"w1sP4_{i}", [128, CT * 1024], BF16)) for i in range(2)]
    w3s = [st.enter_context(sb(f"w3sP4_{i}", [128, CT * 1024], BF16)) for i in range(2)]
    w2s = st.enter_context(sb("w2sP4", [128, 8 * 1024], BF16))
    sil = st.enter_context(sb("silP4", [128, 2 * 512], F32))
    stg = [st.enter_context(sb(f"stgP4_{i}", [128, 2048], F32)) for i in range(2)]

    NS = NE * NSLAB

    def load_w13(gp, dst1, dst3, e, s):
        c0 = s * 1024
        g.dma(gp, "gp", _r3(dst1[:], CT, 1024),
              E["w1_h"][e, :, c0:c0 + 1024].rearrange("(a p) f -> p a f", p=128))
        return g.dma(gp, "gp", _r3(dst3[:], CT, 1024),
                     E["w3_h"][e, :, c0:c0 + 1024].rearrange("(a p) f -> p a f", p=128))

    def load_w2(gp, e, s):
        f0 = s * 1024
        return g.dma(gp, "gp", _r3(w2s[:], 8, 1024),
                     E["w2_h"][e, f0:f0 + 1024, :].rearrange("(a p) d -> p a d", p=128))

    w13_init = [0, 0]

    @g.block.gpsimd
    def _(gp):
        load_w13(gp, w1s[0], w3s[0], 0, 0)
        w13_init[0] = g.c["gp"]
        load_w13(gp, w1s[1], w3s[1], 0, 1)
        w13_init[1] = g.c["gp"]

    bp = dict(g.c)

    def sil_at(kg):
        return bp["se"] + kg + 1

    def hbf_at(si, kl):
        return bp["ve"] + si * 48 + kl + 1

    def yev_at(si, q):
        return bp["ve"] + si * 48 + 32 + q + 1

    def h1_at(si, kl):
        return bp["pe"] + si * 96 + 2 * kl + 1

    def h3_at(si, kl):
        return bp["pe"] + si * 96 + 2 * kl + 2

    def ymm_at(si, q, j):
        return bp["pe"] + si * 96 + 64 + q * 2 + j + 1

    def yend_at(si):
        return bp["pe"] + si * 96 + 96

    def gp_slab_dmas(si):
        return 11 if si + 2 < NS else 9

    def gp_pre(si):
        return bp["gp"] + 16 * sum(gp_slab_dmas(s2) for s2 in range(si))

    def w2_at(si):
        return gp_pre(si) + 16

    def w13_at(si):
        if si == 0:
            return w13_init[0]
        if si == 1:
            return w13_init[1]
        return gp_pre(si - 2) + 16 + 128 + 32

    def acc_at(si, rl):
        return gp_pre(si) + 16 + (rl + 1) * 16

    slabs = [(e, s) for e in range(NE) for s in range(NSLAB)]
    for si, (e, s) in enumerate(slabs):
        buf = s % 2

        @g.block.gpsimd
        def _(gp, e=e, s=s, si=si):
            if si >= 1:
                g.w(gp, "pe", yend_at(si - 1))
            got = load_w2(gp, e, s)
            assert got == w2_at(si), (si, got, w2_at(si))

        @g.block.tensor
        def _(pe, e=e, s=s, si=si, buf=buf):
            g.w(pe, "gp", w13_at(si))
            for ft in range(8):
                for tc in range(NTC):
                    kl = ft * 4 + tc
                    kg = si * 32 + kl
                    if kg >= 2:
                        g.w(pe, "se", sil_at(kg - 2))
                        pv = (si, kl - 2) if kl >= 2 else (si - 1, kl + 30)
                        g.w(pe, "ve", hbf_at(*pv))
                    for ct in range(CT):
                        mm = pe.matmul(pb[tc % 2][:],
                                       w1s[buf][:, ct * 1024 + ft * 128:ct * 1024 + (ft + 1) * 128],
                                       xnT[ct][:, tc * 512:(tc + 1) * 512],
                                       start=(ct == 0), stop=(ct == CT - 1))
                    assert g.i(mm, "pe") == h1_at(si, kl)
                    for ct in range(CT):
                        mm = pe.matmul(pb[2 + tc % 2][:],
                                       w3s[buf][:, ct * 1024 + ft * 128:ct * 1024 + (ft + 1) * 128],
                                       xnT[ct][:, tc * 512:(tc + 1) * 512],
                                       start=(ct == 0), stop=(ct == CT - 1))
                    assert g.i(mm, "pe") == h3_at(si, kl)

        @g.block.scalar
        def _(se, si=si):
            for ft in range(8):
                for tc in range(NTC):
                    kl = ft * 4 + tc
                    kg = si * 32 + kl
                    g.w(se, "pe", h1_at(si, kl))
                    if kg >= 2:
                        pv = (si, kl - 2) if kl >= 2 else (si - 1, kl + 30)
                        g.w(se, "ve", hbf_at(*pv))
                    got = g.i(se.activation(sil[:, (kg % 2) * 512:(kg % 2 + 1) * 512],
                                            pb[tc % 2][:], AF.Silu), "se")
                    assert got == sil_at(kg)

        @g.block.vector
        def _(ve, si=si):
            for ft in range(8):
                for tc in range(NTC):
                    kl = ft * 4 + tc
                    kg = si * 32 + kl
                    g.w(ve, "se", sil_at(kg))
                    g.w(ve, "pe", h3_at(si, kl))
                    if kl == 0 and si >= 1:
                        g.w(ve, "pe", yend_at(si - 1))
                    got = g.i(ve.tensor_mul(h[ft][:, tc * 512:(tc + 1) * 512],
                                            sil[:, (kg % 2) * 512:(kg % 2 + 1) * 512],
                                            pb[2 + tc % 2][:]), "ve")
                    assert got == hbf_at(si, kl)

        @g.block.tensor
        def _(pe, si=si):
            g.w(pe, "gp", w2_at(si))
            for tc in range(NTC):
                g.w(pe, "ve", hbf_at(si, 28 + tc))
                for sub in range(4):
                    q = tc * 4 + sub
                    if q >= 2:
                        g.w(pe, "ve", yev_at(si, q - 2))
                    elif si >= 1:
                        g.w(pe, "ve", yev_at(si - 1, 14 + q))
                    for dc in range(2):
                        bank = 4 + 2 * (q % 2) + dc
                        for ft in range(8):
                            mm = pe.matmul(
                                pb[bank][:],
                                h[ft][:, tc * 512 + sub * 128:tc * 512 + (sub + 1) * 128],
                                w2s[:, ft * 1024 + dc * 512:ft * 1024 + (dc + 1) * 512],
                                start=(ft == 0), stop=(ft == 7))
                        assert g.i(mm, "pe") == ymm_at(si, q, dc)
            assert g.c["pe"] == yend_at(si)

        @g.block.vector
        def _(ve, e=e, si=si):
            for tc in range(NTC):
                for sub in range(4):
                    q = tc * 4 + sub
                    ridx = si * 8 + tc * 2 + sub // 2
                    g.w(ve, "pe", ymm_at(si, q, 1))
                    if sub % 2 == 0 and ridx >= 2:
                        g.w(ve, "gp", acc_at((ridx - 2) // 8, (ridx - 2) % 8))
                    tt = tc * 4 + sub
                    wcol = wiA[:, tt * NE + e:tt * NE + e + 1]
                    for dc in range(2):
                        bank = 4 + 2 * (q % 2) + dc
                        j = (sub % 2) * 2 + dc
                        mm = ve.tensor_single_scalar(
                            stg[ridx % 2][:, j * 512:(j + 1) * 512],
                            pb[bank][:], wcol, ALU.mult)
                    assert g.i(mm, "ve") == yev_at(si, q)

        @g.block.gpsimd
        def _(gp, si=si):
            for tc in range(NTC):
                for r in range(2):
                    rl = tc * 2 + r
                    ridx = si * 8 + rl
                    g.w(gp, "ve", yev_at(si, tc * 4 + r * 2 + 1))
                    q0 = tc * 512 + r * 256
                    got = g.dma(
                        gp, "gp",
                        E["acc0_h"][q0:q0 + 256, :].rearrange(
                            "(s p) (d n) -> p s d n", p=128, n=512),
                        stg[ridx % 2][:].rearrange("p (s d n) -> p s d n", s=2, d=2),
                        accum=None if si == 0 else ALU.add)
                    assert got == acc_at(si, rl), (si, rl, got, acc_at(si, rl))

        if si + 2 < NS:
            @g.block.gpsimd
            def _(gp, si=si, buf=buf):
                g.w(gp, "pe", h3_at(si, 31))
                ne2, ns2 = slabs[si + 2]
                got = load_w13(gp, w1s[buf], w3s[buf], ne2, ns2)
                assert got == w13_at(si + 2), (si, got)

    st.close()



def _p5_attn_l0(g, E):
    nc = g.nc
    pb = E["pb"]
    xnT = E["xnT"]
    rwkv = E["rwkv"]
    cst = E["cst"]
    neg1 = cst[0:1, I_NEG1:I_NEG1 + 1]
    st = ExitStack()
    sb = nc.sbuf_tensor
    wv_sb = [st.enter_context(sb(f"wvP5_{i}", [128, CT * 128], BF16)) for i in range(2)]
    wr_sb = [st.enter_context(sb(f"wrP5_{i}", [128, CT * 128], BF16)) for i in range(2)]
    va = [st.enter_context(sb(f"vaP5_{i}", [128, T], F32)) for i in range(2)]
    ra = [st.enter_context(sb(f"raP5_{i}", [128, T], F32)) for i in range(3)]
    B2 = st.enter_context(sb("B2P5", [128, T], F32))
    B2d = st.enter_context(sb("B2dP5", [128, T], F32))
    Bw = st.enter_context(sb("BwP5", [128, T], F32))
    B3d = [st.enter_context(sb(f"B3P5_{i}", [128, T], F32)) for i in range(2)]
    B4d = [st.enter_context(sb(f"B4P5_{i}", [128, T], F32)) for i in range(2)]
    ac = st.enter_context(sb("acP5", [128, 8 * 6], F32))
    negf = cst[:, I_NEG1:I_NEG1 + 1]

    wload = {}

    def load_wvr(gp, dt):
        rot = dt % 2
        g.dma(gp, "gp", _r3(wv_sb[rot][:], CT, 128),
              E["WvT_h"][:, dt * 128:(dt + 1) * 128].rearrange("(a p) n -> p a n", p=128))
        return g.dma(gp, "gp", _r3(wr_sb[rot][:], CT, 128),
                     E["WrT_h"][:, dt * 128:(dt + 1) * 128].rearrange("(a p) n -> p a n", p=128))

    @g.block.gpsimd
    def _(gp):
        wload[0] = load_wvr(gp, 0)
        wload[1] = load_wvr(gp, 1)

    cd = [0]

    @g.block.vector
    def _(ve):
        ve.tensor_scalar_max(ac[:, 0:8], E["td0c"], 0.0)
        ve.tensor_scalar_min(ac[:, 8:16], E["td0c"], 0.0)
        ve.tensor_scalar_max(ac[:, 16:24], E["tf0c"], 0.0)
        cd[0] = g.i(ve.tensor_scalar_min(ac[:, 24:32], E["tf0c"], 0.0), "ve")

    sed = [0]

    @g.block.scalar
    def _(se):
        g.w(se, "ve", cd[0])
        se.activation(ac[:, 0:8], ac[:, 0:8], AF.Exp, scale=negf)
        se.activation(ac[:, 8:16], ac[:, 8:16], AF.Exp)
        se.activation(ac[:, 16:24], ac[:, 16:24], AF.Exp, scale=negf)
        sed[0] = g.i(se.activation(ac[:, 24:32], ac[:, 24:32], AF.Exp), "se")

    @g.block.vector
    def _(ve):
        g.w(ve, "se", sed[0])
        ve.tensor_scalar_add(ac[:, 32:40], ac[:, 24:32], 1e-6)
        g.i(ve.tensor_mul(ac[:, 40:48], ac[:, 16:24], ac[:, 8:16]), "ve")

    bp = dict(g.c)

    SC = {"ve": {}, "se": {}, "pe": {}, "pc": {}}
    cnt = {"ve": 0, "se": 0, "pe": 0, "pc": 0}

    def _ev(eng, key, n=1):
        cnt[eng] += n
        SC[eng][key] = cnt[eng]

    for _dt in range(CT + 1):
        if _dt < CT:
            for _tc in range(NTC):
                _ev("pe", ("mmv", _dt, _tc))
                _ev("pe", ("mmr", _dt, _tc))
            for _tc in range(NTC):
                _ev("se", ("evv", _dt, _tc))
                _ev("se", ("evr", _dt, _tc))
        if _dt >= 1:
            _d = _dt - 1
            _ev("ve", ("scanB", _d))
            _ev("ve", ("aff", _d))
            _ev("ve", ("tssN", _d))
            _ev("ve", ("sttN", _d))
            _ev("se", ("lnexp", _d))
            _ev("pc", ("mul1", _d))
            _ev("pc", ("rwkv", _d))

    def at(eng, *key):
        return bp[eng] + SC[eng][key]

    E["p5_rwkv_at"] = [at("pc", "rwkv", d) for d in range(CT)]

    def emit_chain(d):
        al = ac[:, 0 + d:1 + d]
        be = ac[:, 8 + d:9 + d]
        e1 = ac[:, 16 + d:17 + d]
        e2 = ac[:, 24 + d:25 + d]
        e2e = ac[:, 32 + d:33 + d]
        e1be = ac[:, 40 + d:41 + d]

        @g.block.vector
        def _(ve, d=d):
            if d >= 2:
                g.w(ve, "se", at("se", "lnexp", d - 2))
                g.w(ve, "pc", at("pc", "mul1", d - 2))
            g.w(ve, "se", at("se", "evv", d, NTC - 1))
            B3_ = B3d[d % 2]
            got = g.i(ve.tensor_tensor_scan(B2d[:], al.broadcast_to([128, T]),
                                            be.broadcast_to([128, T]), 0.0,
                                            ALU.mult, ALU.add), "ve")
            assert got == at("ve", "scanB", d)
            B4_ = B4d[d % 2]
            ve.tensor_scalar(B4_[:, 1:T], B2d[:, 0:T - 1], e1, e2e,
                             ALU.mult, ALU.add)
            got = g.i(ve.tensor_copy(B4_[:, 0:1], e2e), "ve")
            assert got == at("ve", "aff", d)
            ve.tensor_tensor_scan(B2[:], al.broadcast_to([128, T]), va[d % 2][:],
                                  0.0, ALU.mult, ALU.add)
            got = g.i(ve.tensor_single_scalar(B3_[:], va[d % 2][:], e2, ALU.mult),
                      "ve")
            assert got == at("ve", "tssN", d)
            got = g.i(ve.scalar_tensor_tensor(B3_[:, 1:T], B2[:, 0:T - 1], e1be,
                                              B3_[:, 1:T], ALU.mult, ALU.add), "ve")
            assert got == at("ve", "sttN", d)

        @g.block.scalar
        def _(se, d=d):
            g.w(se, "ve", at("ve", "aff", d))
            B4_ = B4d[d % 2]
            se.activation(B4_[:], B4_[:], AF.Ln)
            got = g.i(se.activation(B4_[:], B4_[:], AF.Exp, scale=negf), "se")
            assert got == at("se", "lnexp", d)

    def emit_chain_gp(d):
        @g.block.gpsimd
        def _(gp, d=d):
            g.w(gp, "ve", at("ve", "sttN", d))
            g.w(gp, "se", at("se", "lnexp", d))
            B3_, B4_ = B3d[d % 2], B4d[d % 2]
            g.i(gp.tensor_mul(Bw[:], B3_[:], B4_[:]), "pc")
            got = g.i(gp.tensor_mul(rwkv[d][:], ra[d % 3][:], Bw[:]), "pc")
            assert got == at("pc", "rwkv", d)

    for dt in range(CT):
        if dt >= 1:
            emit_chain_gp(dt - 1)
        if dt + 2 < CT:
            @g.block.gpsimd
            def _(gp, dt=dt):
                g.w(gp, "pe", at("pe", "mmr", dt, NTC - 1))
                wload[dt + 2] = load_wvr(gp, dt + 2)

        @g.block.tensor
        def _(pe, dt=dt):
            g.w(pe, "gp", wload[dt])
            for tc in range(NTC):
                pidx = (dt, tc - 2) if tc >= 2 else ((dt - 1, tc + 2) if dt > 0 else None)
                if pidx is not None:
                    g.w(pe, "se", at("se", "evr", *pidx))
                for ct in range(CT):
                    mm = pe.matmul(pb[tc % 2][:],
                                   wv_sb[dt % 2][:, ct * 128:(ct + 1) * 128],
                                   xnT[ct][:, tc * 512:(tc + 1) * 512],
                                   start=(ct == 0), stop=(ct == CT - 1))
                assert g.i(mm, "pe") == at("pe", "mmv", dt, tc)
                for ct in range(CT):
                    mm = pe.matmul(pb[2 + tc % 2][:],
                                   wr_sb[dt % 2][:, ct * 128:(ct + 1) * 128],
                                   xnT[ct][:, tc * 512:(tc + 1) * 512],
                                   start=(ct == 0), stop=(ct == CT - 1))
                assert g.i(mm, "pe") == at("pe", "mmr", dt, tc)

        @g.block.scalar
        def _(se, dt=dt):
            if dt >= 3:
                g.w(se, "pc", at("pc", "rwkv", dt - 3))
            if dt >= 2:
                g.w(se, "ve", at("ve", "tssN", dt - 2))
            for tc in range(NTC):
                g.w(se, "pe", at("pe", "mmv", dt, tc))
                got = g.i(se.copy(va[dt % 2][:, tc * 512:(tc + 1) * 512],
                                  pb[tc % 2][:]), "se")
                assert got == at("se", "evv", dt, tc)
                g.w(se, "pe", at("pe", "mmr", dt, tc))
                got = g.i(se.activation(ra[dt % 3][:, tc * 512:(tc + 1) * 512],
                                        pb[2 + tc % 2][:], AF.Sigmoid), "se")
                assert got == at("se", "evr", dt, tc)

        if dt >= 1:
            emit_chain(dt - 1)

    emit_chain(CT - 1)
    emit_chain_gp(CT - 1)

    st.close()



def _p6_combine_l0(g, E):
    nc = g.nc
    pb = E["pb"]
    rwkv = E["rwkv"]
    cst = E["cst"]
    neg1 = cst[0:1, I_NEG1:I_NEG1 + 1]
    st = ExitStack()
    sb = nc.sbuf_tensor
    WoTs = st.enter_context(sb("WoTsP6", [128, CT * 1024], BF16))
    xt = [st.enter_context(sb(f"xtP6_{i}", [128, C], F32)) for i in range(3)]
    at = [st.enter_context(sb(f"atP6_{i}", [128, C], F32)) for i in range(3)]
    gt = [st.enter_context(sb(f"gtP6_{i}", [128, C], F32)) for i in range(3)]
    x1b = [st.enter_context(sb(f"x1bP6_{i}", [128, C], BF16)) for i in range(3)]
    x1r = [st.enter_context(sb(f"x1rP6_{i}", [128, C], BF16)) for i in range(3)]
    xq = [st.enter_context(sb(f"xqP6_{i}", [128, C], F32)) for i in range(3)]
    tb = st.enter_context(sb("tbP6", [128, 512], F32))
    m1c, r1c = E["m1c"], E["r1c"]
    sc2, sc3 = E["sc2"], E["sc3"]

    wload = [0]

    @g.block.gpsimd
    def _(gp):
        wload[0] = g.dma(gp, "gp", _r3(WoTs[:], CT, 1024),
                         E["WoT_h"].rearrange("(a p) d -> p a d", p=128))

    ld = {}
    smm = {}
    gdone = {}
    x1done = {}
    sqd = {}
    statd = {}
    stored = {}
    gsto = {}

    for tt in range(TT):
        rot = tt % 3

        @g.block.gpsimd
        def _(gp, tt=tt, rot=rot):
            if tt >= 3:
                g.w(gp, "ve", sqd[tt - 3])
            ld[("x", tt)] = g.dma(gp, "gp", xt[rot][:],
                                  E["x_h"][tt * 128:(tt + 1) * 128, :])

        @g.block.sync
        def _(sy, tt=tt, rot=rot):
            if tt >= 3:
                g.w(sy, "se", gdone[tt - 3])
            ld[("a", tt)] = g.dma(sy, "sy", at[rot][:],
                                  E["acc0_h"][tt * 128:(tt + 1) * 128, :])

        @g.block.tensor
        def _(pe, tt=tt):
            g.w(pe, "gp", wload[0])
            if tt >= 2:
                g.w(pe, "ve", x1done[tt - 2])
            for dc in range(2):
                bank = 2 * (tt % 2) + dc
                for dtc in range(CT):
                    if tt == 0 and dc == 0:
                        g.w(pe, "pc", E["p5_rwkv_at"][dtc])
                    mm = pe.matmul(pb[bank][:],
                                   rwkv[dtc][:, tt * 128:(tt + 1) * 128],
                                   WoTs[:, dtc * 1024 + dc * 512:dtc * 1024 + (dc + 1) * 512],
                                   start=(dtc == 0), stop=(dtc == CT - 1))
            smm[tt] = g.i(mm, "pe")

        @g.block.scalar
        def _(se, tt=tt, rot=rot):
            g.w(se, "sy", ld[("a", tt)])
            if tt >= 3:
                g.w(se, "ve", x1done[tt - 3])
            gdone[tt] = g.i(se.activation(gt[rot][:], at[rot][:], AF.Silu), "se")

        @g.block.vector
        def _(ve, tt=tt, rot=rot):
            g.w(ve, "pe", smm[tt])
            g.w(ve, "se", gdone[tt])
            g.w(ve, "gp", ld[("x", tt)])
            if tt >= 3:
                g.w(ve, "gp", gsto[tt - 3])
            for dc in range(2):
                bank = 2 * (tt % 2) + dc
                ve.tensor_mul(tb[:], pb[bank][:], gt[rot][:, dc * 512:(dc + 1) * 512])
                ve.tensor_add(xt[rot][:, dc * 512:(dc + 1) * 512],
                              xt[rot][:, dc * 512:(dc + 1) * 512], tb[:])
            if tt >= 3:
                g.w(ve, "sy", stored[tt - 3])
            ve.tensor_copy(x1b[rot][:], xt[rot][:])
            ve.tensor_sub(x1r[rot][:], xt[rot][:], x1b[rot][:])
            x1done[tt] = g.i(ve.reduce_sum(m1c[:, tt:tt + 1], xt[rot][:], axis=AX), "ve")

        @g.block.vector
        def _(ve, tt=tt, rot=rot):
            ve.tensor_mul(xq[rot][:], xt[rot][:], xt[rot][:])
            sqd[tt] = statd[tt] = g.i(
                ve.reduce_sum(sc2[:, tt:tt + 1], xq[rot][:], axis=AX), "ve")

        @g.block.gpsimd
        def _(gp, tt=tt, rot=rot):
            g.w(gp, "ve", x1done[tt])
            gsto[tt] = g.dma(gp, "gp", E["x1_h"][tt * 128:(tt + 1) * 128, :], xt[rot][:])

        @g.block.sync
        def _(sy, tt=tt, rot=rot):
            g.w(sy, "ve", x1done[tt])
            g.dma(sy, "sy", E["x1b_h"][tt * 128:(tt + 1) * 128, :], x1b[rot][:])
            stored[tt] = g.dma(sy, "sy", E["x1r_h"][tt * 128:(tt + 1) * 128, :],
                               x1r[rot][:])

    fin = [0]

    @g.block.vector
    def _(ve):
        g.w(ve, "ve", statd[TT - 1])
        ve.tensor_scalar_mul(m1c[:], m1c[:], 1.0 / C)
        ve.tensor_scalar_mul(sc2[:], sc2[:], 1.0 / C)
        ve.drain()
        ve.tensor_mul(sc3[:], m1c[:], m1c[:])
        ve.drain()
        fin[0] = g.i(ve.tensor_sub(sc2[:], sc2[:], sc3[:]), "ve")

    fin2 = [0]

    @g.block.scalar
    def _(se):
        g.w(se, "ve", fin[0])
        se.activation(sc2[:], sc2[:], AF.Sqrt, bias=cst[:, I_EPS:I_EPS + 1])
        se.drain()
        se.activation(sc3[:], sc2[:], AF.Ln)
        se.drain()
        fin2[0] = g.i(se.activation(r1c[:], sc3[:], AF.Exp,
                                    scale=cst[:, I_NEG1:I_NEG1 + 1]), "se")

    _cols_to_hbm_row(g, E, m1c, E["m1r_h"], after_se=fin2[0])
    _cols_to_hbm_row(g, E, r1c, E["r1r_h"], after_se=fin2[0])
    st.close()


def _cols_to_hbm_row(g, E, cols, hbm_row, after_se=0, after_ve=0):
    pb = E["pb"]
    trs = E["trs"]
    done = {}

    @g.block.tensor
    def _(pe):
        g.w(pe, "se", after_se)
        g.w(pe, "ve", after_ve if after_ve else g.c["ve"])
        g.w(pe, "sy", g.c["sy"])
        done["tr"] = g.i(pe.transpose(pb[7][0:16, 0:128], cols[:], E["idn"][:]), "pe")

    @g.block.vector
    def _(ve):
        g.w(ve, "pe", done["tr"])
        done["ev"] = g.i(ve.tensor_copy(trs[:], pb[7][0:16, 0:128]), "ve")

    @g.block.sync
    def _(sy):
        g.w(sy, "ve", done["ev"])
        g.dma(sy, "sy", hbm_row.rearrange("o (a n) -> (o a) n", n=128), trs[:])



def _q12_l1(g, E, l1st, upto):
    nc = g.nc
    pb = E["pb"]
    xnT1 = E["xnT1"]
    mwA = E["mwA"]
    csc = E["csc"]
    sb = nc.sbuf_tensor
    st = l1st
    WeT = [st.enter_context(sb(f"WeTQ2_{i}", [128, CT * 1024], BF16)) for i in range(2)]
    stg = [st.enter_context(sb(f"stgQ2_{i}", [128, 2 * C], F32)) for i in range(2)]
    lgt = st.enter_context(sb("lgtQ2", [128, 2 * NE], F32))
    mwT = st.enter_context(sb("mwTQ2", [8, T], BF16))
    rwf = st.enter_context(sb("rwfQ2", [128, CT * NE], F32))
    m1row = st.enter_context(sb("m1rowQ2", [1, T], F32))
    negS = st.enter_context(sb("negSQ2", [1, NE], F32))
    x1g = [st.enter_context(sb(f"x1gQ2_{i}", [128, C], F32)) for i in range(2)]
    xtr = [st.enter_context(sb(f"xtrQ2_{i}", [128, C], F32)) for i in range(2)]
    mu2c = st.enter_context(sb("mu2cQ12", [128, TT], F32))
    r2c = st.enter_context(sb("r2cQ12", [128, TT], F32))
    s1c = st.enter_context(sb("s1cQ12", [128, TT], F32))
    s2c = st.enter_context(sb("s2cQ12", [128, TT], F32))
    r1c = E["r1c"]
    m1c = E["m1c"]
    ones = E["cst"][:, I_ONE:I_ONE + 1]

    xq = st.enter_context(sb("xqQ12", [128, T], F32))
    q1x = ExitStack()
    xbt = [q1x.enter_context(sb(f"xbtQ1_{i}", [128, T], BF16)) for i in range(4)]
    xrt = [q1x.enter_context(sb(f"xrtQ1_{i}", [128, T], BF16)) for i in range(4)]
    rwfb = q1x.enter_context(sb("rwfbQ1", [128, CT * NE], BF16))
    rwr = q1x.enter_context(sb("rwrQ1", [128, CT * NE], BF16))
    s1tmp = q1x.enter_context(sb("s1tmpQ1", [128, T], F32))

    m1bc = q1x.enter_context(sb("m1bcQ1", [128, T], BF16))
    r1bc = q1x.enter_context(sb("r1bcQ1", [128, T], BF16))
    std = {}
    sld = [0]

    @g.block.sync
    def _(sy):
        g.dma(sy, "sy", _r3(rwf[:], CT, NE),
              E["rw_h"].rearrange("(a p) e -> p a e", p=128))
        sld[0] = g.dma(sy, "sy", m1row[:], E["m1r_h"])

    wld = {}

    @g.block.gpsimd
    def _(gp):
        wld[0] = g.dma(gp, "gp", _r3(WeT[0][:], CT, 1024),
                       E["ew_h"][0].rearrange("(a p) d -> p a d", p=128))
        wld[1] = g.dma(gp, "gp", _r3(WeT[1][:], CT, 1024),
                       E["ew_h"][1].rearrange("(a p) d -> p a d", p=128))

    lgd = {}
    mwd = {}
    trd = {}
    mwTd = {}
    ld1 = {}
    bcd = [0]

    @g.block.gpsimd
    def _(gp):
        g.dma(gp, "gp", m1bc[:], E["m1r_h"].broadcast_to([128, T]))
        bcd[0] = 0
        g.dma(gp, "gp", r1bc[:], E["r1r_h"].broadcast_to([128, T]))

    rbld = [0]
    sS = {}

    @g.block.gpsimd
    def _(gp):
        rbld[0] = g.dma(gp, "gp", _r3(rwfb[:], CT, NE),
                        E["rw_h"].rearrange("(a p) e -> p a e", p=128))

    @g.block.tensor
    def _(pe):
        g.w(pe, "sy", sld[0])
        for ct in range(CT):
            mm = pe.matmul(pb[3][0:1, 0:NE], ones, rwf[:, ct * NE:(ct + 1) * NE],
                           start=(ct == 0), stop=(ct == CT - 1))
        sS["mm"] = g.i(mm, "pe")

    @g.block.vector
    def _(ve):
        g.w(ve, "pe", sS["mm"])
        g.w(ve, "gp", rbld[0])
        ve.tensor_scalar_mul(negS[:], pb[3][0:1, 0:NE], -1.0)
        ve.drain()
        sS["neg"] = g.i(ve.tensor_sub(rwr[:], rwf[:], rwfb[:]), "ve")

    q1t = {}
    q1v = {}
    subd = {}
    lgmm = {}

    for ct in range(CT):
        @g.block.sync
        def _(sy, ct=ct):
            if ct >= 4:
                g.w(sy, "pc", q1v[ct - 4])
                g.w(sy, "pe", lgmm[ct // 2 - 2])
            g.dma(sy, "sy", xbt[ct % 4][:],
                  E["x1b_h"][:, ct * 128:(ct + 1) * 128], transpose=True)
            q1t[ct] = g.dma(sy, "sy", xrt[ct % 4][:],
                            E["x1r_h"][:, ct * 128:(ct + 1) * 128], transpose=True)

        @g.block.gpsimd
        def _(gp, ct=ct):
            g.w(gp, "sy", q1t[ct])
            subd[ct] = g.i(gp.tensor_sub(s1tmp[:], xbt[ct % 4][:], m1bc[:]), "pc")
            q1v[ct] = g.i(gp.tensor_mul(xnT1[ct][:], s1tmp[:], r1bc[:]), "pc")

        if ct % 2 == 1:
            p = ct // 2

            @g.block.tensor
            def _(pe, ct=ct, p=p):
                g.w(pe, "sy", q1t[ct])
                if p == 0:
                    g.w(pe, "ve", sS["neg"])
                for tt in range(TT):
                    for cc in (ct - 1, ct):
                        last = (cc == ct and p != 3)
                        pe.matmul(pb[4 + p][:, tt * NE:(tt + 1) * NE],
                                  xbt[cc % 4][:, tt * 128:(tt + 1) * 128],
                                  rwfb[:, cc * NE:(cc + 1) * NE],
                                  start=(cc == ct - 1), stop=False)
                        pe.matmul(pb[4 + p][:, tt * NE:(tt + 1) * NE],
                                  xbt[cc % 4][:, tt * 128:(tt + 1) * 128],
                                  rwr[:, cc * NE:(cc + 1) * NE],
                                  start=False, stop=False)
                        mm = pe.matmul(pb[4 + p][:, tt * NE:(tt + 1) * NE],
                                       xrt[cc % 4][:, tt * 128:(tt + 1) * 128],
                                       rwfb[:, cc * NE:(cc + 1) * NE],
                                       start=False, stop=last)
                    if p == 3:
                        mm = pe.matmul(pb[4 + p][:, tt * NE:(tt + 1) * NE],
                                       m1row[0:1, tt * 128:(tt + 1) * 128],
                                       negS[0:1, :], start=False, stop=True)
                lgmm[p] = g.i(mm, "pe")

    bmm = {}
    bev = {}
    acc_hist = []
    std = {}
    for tt in range(TT):
        rot = tt % 2
        co = (tt % 2) * 16

        @g.block.sync
        def _(sy, tt=tt, rot=rot):
            if tt >= 2:
                g.w(sy, "ve", std[("sq", tt - 2)])
            ld1[tt] = g.dma(sy, "sy", x1g[rot][:],
                            E["x1_h"][tt * 128:(tt + 1) * 128, :])

        @g.block.vector
        def _(ve, tt=tt, rot=rot):
            g.w(ve, "sy", ld1[tt])
            std[("s1", tt)] = g.i(ve.reduce_sum(s1c[:, tt:tt + 1], x1g[rot][:],
                                                axis=AX), "ve")
            std[("sq", tt)] = g.i(ve.tensor_mul(xq[:, rot * C:(rot + 1) * C],
                                                x1g[rot][:], x1g[rot][:]), "ve")
            std[("s2", tt)] = g.i(ve.reduce_sum(s2c[:, tt:tt + 1],
                                                xq[:, rot * C:(rot + 1) * C],
                                                axis=AX), "ve")

        @g.block.vector
        def _(ve, tt=tt, co=co):
            lg = lgt[:, (tt % 2) * NE:(tt % 2) * NE + NE]
            g.w(ve, "pe", lgmm[3])
            if tt >= 2:
                g.w(ve, "ve", mwd[tt - 2])
            ve.tensor_copy(lg, pb[4][:, tt * NE:(tt + 1) * NE])
            ve.drain()
            ve.tensor_add(lg, lg, pb[5][:, tt * NE:(tt + 1) * NE])
            ve.drain()
            ve.tensor_add(lg, lg, pb[6][:, tt * NE:(tt + 1) * NE])
            ve.drain()
            ve.tensor_add(lg, lg, pb[7][:, tt * NE:(tt + 1) * NE])
            ve.drain()
            mwd[("s", tt)] = g.i(ve.scalar_tensor_tensor(
                lg, lg, r1c[:, tt:tt + 1], E["rbbc"][:], ALU.mult, ALU.add), "ve")
            ve.drain()
            ve.reduce_max(csc[:, co:co + 1], lg, axis=AX)
            ve.drain()
            mwd[("m", tt)] = g.i(ve.tensor_scalar_mul(csc[:, co + 1:co + 2],
                                                      csc[:, co:co + 1], -1.0), "ve")

        @g.block.scalar
        def _(se, tt=tt, co=co):
            lg = lgt[:, (tt % 2) * NE:(tt % 2) * NE + NE]
            g.w(se, "ve", mwd[("m", tt)])
            mwd[("e", tt)] = g.i(
                se.activation(csc[:, co + 2:co + 10], lg, AF.Exp,
                              bias=csc[:, co + 1:co + 2]), "se")

        @g.block.vector
        def _(ve, tt=tt, co=co):
            lg = lgt[:, (tt % 2) * NE:(tt % 2) * NE + NE]
            g.w(ve, "se", mwd[("e", tt)])
            ve.reduce_sum(csc[:, co + 10:co + 11], csc[:, co + 2:co + 10], axis=AX)
            ve.drain()
            ve.reciprocal(csc[:, co + 11:co + 12], csc[:, co + 10:co + 11])
            ve.drain()
            mwd[tt] = g.i(ve.tensor_scalar(mwA[:, tt * NE:(tt + 1) * NE], lg,
                                           csc[:, co:co + 1], csc[:, co + 11:co + 12],
                                           ALU.is_equal, ALU.mult), "ve")

    if upto == "Q1":
        return

    for tt in range(TT):
        @g.block.tensor
        def _(pe, tt=tt):
            g.w(pe, "ve", mwd[tt])
            if tt == 0:
                g.w(pe, "ve", mwd[("s", TT - 1)])
            if tt >= 1:
                g.w(pe, "ve", mwTd[tt - 1])
            trd[tt] = g.i(pe.transpose(pb[4][0:8, 0:128],
                                       mwA[:, tt * NE:(tt + 1) * NE],
                                       E["idn"][:]), "pe")

        @g.block.vector
        def _(ve, tt=tt):
            g.w(ve, "pe", trd[tt])
            mwTd[tt] = g.i(ve.tensor_copy(mwT[:, tt * 128:(tt + 1) * 128],
                                          pb[4][0:8, 0:128]), "ve")

        @g.block.tensor
        def _(pe, tt=tt):
            g.w(pe, "ve", mwTd[tt])
            if tt >= 1:
                g.w(pe, "ve", bev[tt - 1])
            for dc in range(2):
                bmm[(tt, dc)] = g.i(pe.matmul(
                    pb[6 + dc][:],
                    mwT[0:8, tt * 128:(tt + 1) * 128],
                    E["ebb"][0:8, dc * 512:(dc + 1) * 512],
                    start=True, stop=True), "pe")

        @g.block.vector
        def _(ve, tt=tt):
            g.w(ve, "pe", bmm[(tt, 1)])
            slot = (tt // 2) % 2
            if tt % 2 == 0 and len(acc_hist) >= 2:
                g.w(ve, "gp", acc_hist[-2])
            for dc in range(2):
                mm = ve.tensor_copy(
                    stg[slot][:, (tt % 2) * C + dc * 512:(tt % 2) * C + (dc + 1) * 512],
                    pb[6 + dc][:])
            bev[tt] = g.i(mm, "ve")

        if tt % 2 == 1:
            @g.block.gpsimd
            def _(gp, tt=tt):
                g.w(gp, "ve", bev[tt])
                slot = (tt // 2) % 2
                acc_hist.append(g.dma(
                    gp, "gp",
                    E["acc1_h"][(tt - 1) * 128:(tt + 1) * 128, :].rearrange(
                        "(s p) d -> p s d", p=128),
                    _r3(stg[slot][:], 2, C)))

    fin = [0]

    @g.block.vector
    def _(ve):
        g.w(ve, "ve", std[("s2", TT - 1)])
        ve.tensor_scalar_mul(s1c[:], s1c[:], 1.0 / C)
        ve.drain()
        ve.tensor_sub(mu2c[:], s1c[:], m1c[:])
        ve.drain()
        ve.tensor_mul(mu2c[:], mu2c[:], r1c[:])
        ve.drain()
        ve.tensor_scalar_mul(s2c[:], s2c[:], 1.0 / C)
        ve.drain()
        ve.tensor_mul(s1c[:], s1c[:], s1c[:])
        ve.drain()
        ve.tensor_sub(s2c[:], s2c[:], s1c[:])
        ve.drain()
        ve.tensor_mul(s2c[:], s2c[:], r1c[:])
        ve.drain()
        ve.tensor_mul(s2c[:], s2c[:], r1c[:])
        ve.drain()
        ve.tensor_mul(s1c[:], mu2c[:], mu2c[:])
        ve.drain()
        fin[0] = g.i(ve.tensor_sub(s2c[:], s2c[:], s1c[:]), "ve")

    fin2 = [0]

    @g.block.scalar
    def _(se):
        g.w(se, "ve", fin[0])
        se.activation(s2c[:], s2c[:], AF.Sqrt, bias=E["cst"][:, I_EPS:I_EPS + 1])
        se.drain()
        se.activation(s1c[:], s2c[:], AF.Ln)
        se.drain()
        fin2[0] = g.i(se.activation(r2c[:], s1c[:], AF.Exp,
                                    scale=E["cst"][:, I_NEG1:I_NEG1 + 1]), "se")

    fin3 = [0]

    @g.block.vector
    def _(ve):
        g.w(ve, "se", fin2[0])
        ve.tensor_mul(s1c[:], r1c[:], r2c[:])
        ve.drain()
        ve.tensor_mul(s2c[:], m1c[:], r1c[:])
        ve.drain()
        ve.tensor_add(s2c[:], s2c[:], mu2c[:])
        ve.drain()
        fin3[0] = g.i(ve.tensor_mul(s2c[:], s2c[:], r2c[:]), "ve")

    _cols_to_hbm_row(g, E, s1c, E["p4r_h"], after_se=fin2[0], after_ve=fin3[0])
    _cols_to_hbm_row(g, E, s2c, E["q4r_h"], after_se=fin2[0], after_ve=fin3[0])

    q1x.close()
    xbt2 = [st.enter_context(sb(f"xbt2Q12_{i}", [128, T], BF16)) for i in range(2)]
    Pbc = st.enter_context(sb("PbcQ12", [128, T], F32))
    Qbc = st.enter_context(sb("QbcQ12", [128, T], F32))

    bcd2 = [0]

    @g.block.sync
    def _(sy):
        g.w(sy, "pc", q1v[CT - 1])
        g.w(sy, "pe", lgmm[3])
        g.dma(sy, "sy", Pbc[:], E["p4r_h"].broadcast_to([128, T]))
        bcd2[0] = g.dma(sy, "sy", Qbc[:], E["q4r_h"].broadcast_to([128, T]))

    ymm = {}
    yev = {}
    ve_gate_done = g.c["ve"]
    for e in range(NE):
        for tt in range(TT):
            gbase = e * TT + tt

            @g.block.tensor
            def _(pe, e=e, tt=tt, gbase=gbase):
                if tt == 0:
                    g.w(pe, "gp", wld[e])
                    if e == 0:
                        g.w(pe, "pc", q1v[CT - 1])
                        g.w(pe, "ve", sS["neg"])
                for dc in range(2):
                    gi = gbase * 2 + dc
                    if gi >= 4:
                        g.w(pe, "ve", yev[(gi - 4) // 2])
                    for ct in range(CT):
                        mm = pe.matmul(
                            pb[gi % 4][:],
                            xnT1[ct][:, tt * 128:(tt + 1) * 128],
                            WeT[e % 2][:, ct * 1024 + dc * 512:ct * 1024 + (dc + 1) * 512],
                            start=(ct == 0), stop=(ct == CT - 1))
                    ymm[gi] = g.i(mm, "pe")

            @g.block.vector
            def _(ve, e=e, tt=tt, gbase=gbase):
                g.w(ve, "pe", ymm[gbase * 2 + 1])
                pair = gbase // 2
                slot = pair % 2
                if tt % 2 == 0:
                    if pair >= 2:
                        g.w(ve, "gp", acc_hist[8 + pair - 2])
                    else:
                        g.w(ve, "gp", acc_hist[6 + pair])
                wcol = mwA[:, tt * NE + e:tt * NE + e + 1]
                for dc in range(2):
                    gi = gbase * 2 + dc
                    mm = ve.tensor_single_scalar(
                        stg[slot][:, (tt % 2) * C + dc * 512:(tt % 2) * C + (dc + 1) * 512],
                        pb[gi % 4][:], wcol, ALU.mult)
                yev[gbase] = g.i(mm, "ve")

            if tt % 2 == 1:
                @g.block.gpsimd
                def _(gp, e=e, tt=tt, gbase=gbase):
                    g.w(gp, "ve", yev[gbase])
                    slot = (gbase // 2) % 2
                    acc_hist.append(g.dma(
                        gp, "gp",
                        E["acc1_h"][(tt - 1) * 128:(tt + 1) * 128, :].rearrange(
                            "(s p) d -> p s d", p=128),
                        _r3(stg[slot][:], 2, C), accum=ALU.add))

        if e + 2 < NE:
            @g.block.gpsimd
            def _(gp, e=e):
                g.w(gp, "pe", ymm[(e * TT + TT - 1) * 2 + 1])
                wld[e + 2] = g.dma(gp, "gp", _r3(WeT[e % 2][:], CT, 1024),
                                   E["ew_h"][e + 2].rearrange("(a p) d -> p a d", p=128))

        if e == 0:
            b1d = {}
            b1v = {}
            for ct in range(CT):
                @g.block.sync
                def _(sy, ct=ct):
                    if ct >= 2:
                        g.w(sy, "ve", b1v[ct - 2])
                    b1d[ct] = g.dma(sy, "sy", xbt2[ct % 2][:],
                                    E["x1b_h"][:, ct * 128:(ct + 1) * 128],
                                    transpose=True)

                @g.block.vector
                def _(ve, ct=ct):
                    g.w(ve, "sy", max(b1d[ct], bcd2[0]))
                    ve.tensor_mul(xq[:], xbt2[ct % 2][:], Pbc[:])
                    ve.tensor_sub(xq[:], xq[:], Qbc[:])
                    b1v[ct] = g.i(ve.tensor_scalar(E["xn1T"][ct][:], xq[:],
                                                   E["g1c"][:, ct:ct + 1],
                                                   E["b1c"][:, ct:ct + 1],
                                                   ALU.mult, ALU.add), "ve")



def _q3_timemix_l1(g, E):
    nc = g.nc
    pb = E["pb"]
    cst = E["cst"]
    neg1 = cst[0:1, I_NEG1:I_NEG1 + 1]
    rwkv1 = E["rwkv1"]
    st = ExitStack()
    sb = nc.sbuf_tensor

    st3 = ExitStack()
    wp = [[st3.enter_context(sb(f"wpQ3_{w}_{i}", [128, CT * 128], BF16))
           for i in range(2)] for w in range(4)]
    ra3 = [st3.enter_context(sb(f"ra3Q3_{i}", [128, T], F32)) for i in range(3)]
    va3 = [st3.enter_context(sb(f"va3Q3_{i}", [128, T], F32)) for i in range(3)]
    ks3 = [st3.enter_context(sb(f"ks3Q3_{i}", [128, T], F32)) for i in range(2)]
    ewb = st3.enter_context(sb("ewbQ3", [128, T], F32))
    S1 = [st3.enter_context(sb(f"S1Q3_{i}", [128, T], F32)) for i in range(2)]
    S2 = st3.enter_context(sb("S2Q3", [128, T], F32))
    S2p = st3.enter_context(sb("S2pQ3", [128, T], F32))
    S3 = st3.enter_context(sb("S3Q3", [128, T], F32))
    S2d = st3.enter_context(sb("S2dQ3", [128, T], F32))
    dce = st3.enter_context(sb("dceQ3", [128, CT], F32))

    @g.block.scalar
    def _(se):
        g.i(se.activation(dce[:], E["td1c"], AF.Exp), "se")

    wsrc = {0: "Wr1_h", 1: "Ww1_h", 2: "Wk1_h", 3: "Wv1_h"}
    wld = {}

    def load_wp(gp, dt):
        rot = dt % 2
        for w in range(4):
            last = g.dma(gp, "gp", _r3(wp[w][rot][:], CT, 128),
                         E[wsrc[w]][:, dt * 128:(dt + 1) * 128].rearrange(
                             "(a p) n -> p a n", p=128))
        return last

    @g.block.gpsimd
    def _(gp):
        wld[0] = load_wp(gp, 0)
        wld[1] = load_wp(gp, 1)

    bp = dict(g.c)

    SC = {"ve": {}, "se": {}, "pe": {}, "pc": {}}
    cnt = {"ve": 0, "se": 0, "pe": 0, "pc": 0}

    def _ev(eng, key, n=1):
        cnt[eng] += n
        SC[eng][key] = cnt[eng]

    for _dt in range(CT + 1):
        _d = _dt - 1
        if _dt < CT:
            for _w in range(4):
                for _tc in range(NTC):
                    _ev("pe", ("mmp", _dt, _w, _tc))
            for _tc in range(NTC):
                _ev("se", ("evr", _dt, _tc))
            if _dt >= 1:
                _ev("se", ("e", _d))
            for _tc in range(NTC):
                _ev("se", ("evw", _dt, _tc))
            if _dt >= 1:
                _ev("se", ("lnexp", _d))
            for _tc in range(NTC):
                _ev("se", ("vcp", _dt, _tc))
            if _dt >= 1:
                _ev("ve", ("pm0", _d))
                _ev("ve", ("scanA", _d))
                _ev("ve", ("scanB", _d))
            for _tc in range(NTC):
                _ev("ve", ("sub", _dt, _tc))
        else:
            _ev("se", ("e", _d))
            _ev("se", ("lnexp", _d))
            _ev("ve", ("pm0", _d))
            _ev("ve", ("scanA", _d))
            _ev("ve", ("scanB", _d))
        if _dt >= 1:
            _ev("pc", ("mul1", _d))
            _ev("pc", ("rwkv", _d))

    def at(eng, *key):
        return bp[eng] + SC[eng][key]

    E["q3_rwkv_at"] = [at("pc", "rwkv", d) for d in range(CT)]
    bank_consumer = {0: ("se", "evr"), 1: ("se", "evw"),
                     2: ("ve", "sub"), 3: ("se", "vcp")}

    def emit_e(se, d):
        g.w(se, "ve", at("ve", "sub", d, NTC - 1))
        if d >= 2:
            g.w(se, "ve", at("ve", "scanB", d - 2))
        assert g.i(se.activation(S1[d % 2][:], ks3[d % 2][:], AF.Exp), "se") \
            == at("se", "e", d)

    def emit_lnexp(se, d):
        g.w(se, "ve", at("ve", "scanB", d))
        se.activation(S2d[:], S2d[:], AF.Ln, bias=cst[:, I_E8:I_E8 + 1])
        got = g.i(se.activation(S2d[:], S2d[:], AF.Exp,
                                scale=cst[:, I_NEG1:I_NEG1 + 1]), "se")
        assert got == at("se", "lnexp", d)

    def emit_scans(ve, d):
        g.w(ve, "se", at("se", "e", d))
        got = g.i(ve.tensor_mul(S2p[:], S1[d % 2][:], va3[d % 3][:]), "ve")
        assert got == at("ve", "pm0", d)
        if d >= 1:
            g.w(ve, "se", at("se", "lnexp", d - 1))
            g.w(ve, "pc", at("pc", "mul1", d - 1))
        dcol = dce[:, d:d + 1]
        got = g.i(ve.tensor_tensor_scan(S3[:], dcol.broadcast_to([128, T]),
                                        S2p[:], 0.0, ALU.mult, ALU.add), "ve")
        assert got == at("ve", "scanA", d)
        got = g.i(ve.tensor_tensor_scan(S2d[:], dcol.broadcast_to([128, T]),
                                        S1[d % 2][:], 0.0, ALU.mult, ALU.add), "ve")
        assert got == at("ve", "scanB", d)

    def emit_chain_gp(d):
        @g.block.gpsimd
        def _(gp, d=d):
            g.w(gp, "se", at("se", "lnexp", d))
            g.i(gp.tensor_mul(S2[:], S3[:], S2d[:]), "pc")
            got = g.i(gp.tensor_mul(rwkv1[d][:], ra3[d % 3][:], S2[:]), "pc")
            assert got == at("pc", "rwkv", d)

    for dt in range(CT):
        if dt >= 1:
            emit_chain_gp(dt - 1)
        if dt + 2 < CT:
            @g.block.gpsimd
            def _(gp, dt=dt):
                g.w(gp, "pe", at("pe", "mmp", dt, 3, NTC - 1))
                wld[dt + 2] = load_wp(gp, dt + 2)

        @g.block.tensor
        def _(pe, dt=dt):
            g.w(pe, "gp", wld[dt])
            for w in range(4):
                for tc in range(NTC):
                    bank = 2 * w + tc % 2
                    pidx = (dt, tc - 2) if tc >= 2 else ((dt - 1, tc + 2) if dt > 0 else None)
                    if pidx is not None:
                        sname, ev = bank_consumer[w]
                        g.w(pe, sname, at(sname, ev, *pidx))
                    for ct in range(CT):
                        mm = pe.matmul(pb[bank][:],
                                       wp[w][dt % 2][:, ct * 128:(ct + 1) * 128],
                                       E["xn1T"][ct][:, tc * 512:(tc + 1) * 512],
                                       start=(ct == 0), stop=(ct == CT - 1))
                    assert g.i(mm, "pe") == at("pe", "mmp", dt, w, tc)

        @g.block.scalar
        def _(se, dt=dt):
            if dt >= 3:
                g.w(se, "pc", at("pc", "rwkv", dt - 3))
                g.w(se, "ve", at("ve", "pm0", dt - 3))
            for tc in range(NTC):
                g.w(se, "pe", at("pe", "mmp", dt, 0, tc))
                got = g.i(se.activation(ra3[dt % 3][:, tc * 512:(tc + 1) * 512],
                                        pb[tc % 2][:], AF.Sigmoid), "se")
                assert got == at("se", "evr", dt, tc)
            if dt >= 1:
                emit_e(se, dt - 1)
                g.w(se, "ve", at("ve", "sub", dt - 1, NTC - 1))
            for tc in range(NTC):
                g.w(se, "pe", at("pe", "mmp", dt, 1, tc))
                got = g.i(se.activation(ewb[:, tc * 512:(tc + 1) * 512],
                                        pb[2 + tc % 2][:], AF.Exp), "se")
                assert got == at("se", "evw", dt, tc)
            if dt >= 1:
                emit_lnexp(se, dt - 1)
            for tc in range(NTC):
                g.w(se, "pe", at("pe", "mmp", dt, 3, tc))
                got = g.i(se.copy(va3[dt % 3][:, tc * 512:(tc + 1) * 512],
                                  pb[6 + tc % 2][:]), "se")
                assert got == at("se", "vcp", dt, tc)

        @g.block.vector
        def _(ve, dt=dt):
            if dt >= 1:
                emit_scans(ve, dt - 1)
            if dt >= 2:
                g.w(ve, "se", at("se", "e", dt - 2))
            for tc in range(NTC):
                g.w(ve, "pe", at("pe", "mmp", dt, 2, tc))
                g.w(ve, "se", at("se", "evw", dt, tc))
                got = g.i(ve.tensor_sub(ks3[dt % 2][:, tc * 512:(tc + 1) * 512],
                                        pb[4 + tc % 2][:],
                                        ewb[:, tc * 512:(tc + 1) * 512]), "ve")
                assert got == at("ve", "sub", dt, tc)

    @g.block.scalar
    def _(se):
        emit_e(se, CT - 1)
        emit_lnexp(se, CT - 1)

    @g.block.vector
    def _(ve):
        emit_scans(ve, CT - 1)

    emit_chain_gp(CT - 1)

    st3.close()
    st.close()



def _q4a_xx(g, E):
    nc = g.nc
    pb = E["pb"]
    cst = E["cst"]
    neg1 = cst[0:1, I_NEG1:I_NEG1 + 1]
    rwkv1 = E["rwkv1"]
    m1c, r1c = E["m1c"], E["r1c"]
    sc2, sc3 = E["sc2"], E["sc3"]
    st = ExitStack()
    sb = nc.sbuf_tensor
    Wo1s = st.enter_context(sb("Wo1sQ4", [128, CT * 1024], BF16))
    x1t = [st.enter_context(sb(f"x1tQ4_{i}", [128, C], F32)) for i in range(3)]
    xxt = [st.enter_context(sb(f"xxtQ4_{i}", [128, C], F32)) for i in range(3)]
    xxb = [st.enter_context(sb(f"xxbQ4_{i}", [128, C], BF16)) for i in range(3)]
    xq = [st.enter_context(sb(f"xqQ4_{i}", [128, C], F32)) for i in range(3)]
    m3c = st.enter_context(sb("m3cQ4", [128, TT], F32))
    r3c = st.enter_context(sb("r3cQ4", [128, TT], F32))

    wload = [0]

    @g.block.gpsimd
    def _(gp):
        wload[0] = g.dma(gp, "gp", _r3(Wo1s[:], CT, 1024),
                         E["Wo1_h"].rearrange("(a p) d -> p a d", p=128))

    ld = {}
    tmm = {}
    xxd = {}
    sqd = {}
    statd = {}
    stored = {}
    gsto = {}

    for tt in range(TT):
        rot = tt % 3

        @g.block.gpsimd
        def _(gp, tt=tt, rot=rot):
            if tt >= 3:
                g.w(gp, "ve", xxd[tt - 3])
            ld[tt] = g.dma(gp, "gp", x1t[rot][:], E["x1_h"][tt * 128:(tt + 1) * 128, :])

        @g.block.tensor
        def _(pe, tt=tt):
            g.w(pe, "gp", wload[0])
            if tt >= 2:
                g.w(pe, "ve", xxd[tt - 2])
            for dc in range(2):
                bank = 2 * (tt % 2) + dc
                for dtc in range(CT):
                    mm = pe.matmul(pb[bank][:],
                                   rwkv1[dtc][:, tt * 128:(tt + 1) * 128],
                                   Wo1s[:, dtc * 1024 + dc * 512:dtc * 1024 + (dc + 1) * 512],
                                   start=(dtc == 0), stop=(dtc == CT - 1))
            tmm[tt] = g.i(mm, "pe")

        @g.block.vector
        def _(ve, tt=tt, rot=rot):
            g.w(ve, "gp", ld[tt])
            g.w(ve, "pe", tmm[tt])
            if tt >= 3:
                g.w(ve, "ve", sqd[tt - 3])
                g.w(ve, "gp", gsto[tt - 3])
                g.w(ve, "sy", stored[tt - 3])
            ve.tensor_scalar(xxt[rot][:], x1t[rot][:], m1c[:, tt:tt + 1],
                             r1c[:, tt:tt + 1], ALU.subtract, ALU.mult)
            for dc in range(2):
                bank = 2 * (tt % 2) + dc
                ve.tensor_add(xxt[rot][:, dc * 512:(dc + 1) * 512],
                              xxt[rot][:, dc * 512:(dc + 1) * 512], pb[bank][:])
            ve.tensor_copy(xxb[rot][:], xxt[rot][:])
            xxd[tt] = g.i(ve.reduce_sum(m3c[:, tt:tt + 1], xxt[rot][:], axis=AX), "ve")

        @g.block.vector
        def _(ve, tt=tt, rot=rot):
            ve.tensor_mul(xq[rot][:], xxt[rot][:], xxt[rot][:])
            sqd[tt] = statd[tt] = g.i(
                ve.reduce_sum(sc2[:, tt:tt + 1], xq[rot][:], axis=AX), "ve")

        @g.block.gpsimd
        def _(gp, tt=tt, rot=rot):
            g.w(gp, "ve", xxd[tt])
            gsto[tt] = g.dma(gp, "gp", E["xx_h"][tt * 128:(tt + 1) * 128, :], xxt[rot][:])

        @g.block.sync
        def _(sy, tt=tt, rot=rot):
            g.w(sy, "ve", xxd[tt])
            stored[tt] = g.dma(sy, "sy", E["xxb_h"][tt * 128:(tt + 1) * 128, :],
                               xxb[rot][:])

    fin = [0]

    @g.block.vector
    def _(ve):
        g.w(ve, "ve", statd[TT - 1])
        ve.tensor_scalar_mul(m3c[:], m3c[:], 1.0 / C)
        ve.tensor_scalar_mul(sc2[:], sc2[:], 1.0 / C)
        ve.drain()
        ve.tensor_mul(sc3[:], m3c[:], m3c[:])
        ve.drain()
        fin[0] = g.i(ve.tensor_sub(sc2[:], sc2[:], sc3[:]), "ve")

    fin2 = [0]

    @g.block.scalar
    def _(se):
        g.w(se, "ve", fin[0])
        se.activation(sc2[:], sc2[:], AF.Sqrt, bias=cst[:, I_EPS:I_EPS + 1])
        se.drain()
        se.activation(sc3[:], sc2[:], AF.Ln)
        se.drain()
        fin2[0] = g.i(se.activation(r3c[:], sc3[:], AF.Exp,
                                    scale=cst[:, I_NEG1:I_NEG1 + 1]), "se")

    _cols_to_hbm_row(g, E, m3c, E["m3r_h"], after_se=fin2[0])
    _cols_to_hbm_row(g, E, r3c, E["r3r_h"], after_se=fin2[0])
    st.close()



def _q4b_final(g, E):
    nc = g.nc
    pb = E["pb"]
    st = ExitStack()
    sb = nc.sbuf_tensor

    xxnT = [st.enter_context(sb(f"xxnT_{i}", [128, T], BF16)) for i in range(CT)]
    st2 = ExitStack()
    xbt = [st2.enter_context(sb(f"xbtQ4_{i}", [128, T], BF16)) for i in range(2)]
    s1 = st2.enter_context(sb("s1Q4", [128, T], F32))
    m3bc = st2.enter_context(sb("m3bcQ4", [128, T], F32))
    r3bc = st2.enter_context(sb("r3bcQ4", [128, T], F32))

    bcd = [0]

    @g.block.sync
    def _(sy):
        g.dma(sy, "sy", m3bc[:], E["m3r_h"].broadcast_to([128, T]))
        bcd[0] = g.dma(sy, "sy", r3bc[:], E["r3r_h"].broadcast_to([128, T]))

    trd = {}
    vd = {}
    for ct in range(CT):
        @g.block.sync
        def _(sy, ct=ct):
            if ct >= 2:
                g.w(sy, "ve", vd[ct - 2])
            trd[ct] = g.dma(sy, "sy", xbt[ct % 2][:],
                            E["xxb_h"][:, ct * 128:(ct + 1) * 128], transpose=True)

        @g.block.vector
        def _(ve, ct=ct):
            g.w(ve, "sy", max(trd[ct], bcd[0]))
            ve.tensor_sub(s1[:], xbt[ct % 2][:], m3bc[:])
            ve.tensor_mul(s1[:], s1[:], r3bc[:])
            vd[ct] = g.i(ve.tensor_scalar(xxnT[ct][:], s1[:], E["g2c"][:, ct:ct + 1],
                                          E["b2c"][:, ct:ct + 1], ALU.mult, ALU.add), "ve")

    g.barrier()
    st2.close()

    st3 = ExitStack()
    WrcS = st3.enter_context(sb("WrcSQ4", [128, CT * 1024], BF16))
    wky = [st3.enter_context(sb(f"wkyQ4_{i}", [128, CT * 512], BF16)) for i in range(2)]
    wvl = [st3.enter_context(sb(f"wvlQ4_{i}", [128, 4 * 1024], BF16)) for i in range(2)]
    kcm = [st3.enter_context(sb(f"kcmQ4_{i}", [128, 512], BF16)) for i in range(32)]
    rl = st3.enter_context(sb("rlQ4", [128, 2 * 512], F32))
    rcm = st3.enter_context(sb("rcmQ4", [128, 4 * C], F32))
    xxt2 = [st3.enter_context(sb(f"xxt2Q4_{i}", [128, C], F32)) for i in range(2)]
    a1t = [st3.enter_context(sb(f"a1tQ4_{i}", [128, C], F32)) for i in range(2)]
    g1t = [st3.enter_context(sb(f"g1tQ4_{i}", [128, C], F32)) for i in range(2)]
    x1t = [st3.enter_context(sb(f"x1cQ4_{i}", [128, C], F32)) for i in range(2)]
    tb = st3.enter_context(sb("tbQ4", [128, 512], F32))

    wrcl = [0]

    @g.block.gpsimd
    def _(gp):
        wrcl[0] = g.dma(gp, "gp", _r3(WrcS[:], CT, 1024),
                        E["Wrc_h"].rearrange("(a p) d -> p a d", p=128))

    def load_wky(gp, fg, rot):
        return g.dma(gp, "gp", _r3(wky[rot][:], CT, 512),
                     E["Wky_h"][:, fg * 512:(fg + 1) * 512].rearrange(
                         "(a p) f -> p a f", p=128))

    def load_wvl(gp, fg, rot):
        return g.dma(gp, "gp", _r3(wvl[rot][:], 4, 1024),
                     E["Wvl_h"][fg * 512:(fg + 1) * 512, :].rearrange(
                         "(a p) d -> p a d", p=128))

    kyd = {}
    vld = {}
    rvd = {}
    rmm = {}
    red = {}
    kmm = {}
    krl = {}
    kbf = {}
    vmm = {}
    ldd = {}
    sild = {}
    cmbd = {}
    sto = {}

    for tc in range(NTC):
        for gi in range(8):
            sub, dc = gi // 2, gi % 2

            @g.block.tensor
            def _(pe, tc=tc, gi=gi, sub=sub, dc=dc):
                if gi == 0:
                    g.w(pe, "gp", wrcl[0])
                    if tc > 0:
                        g.w(pe, "ve", rvd[tc - 1])
                pv = (tc, gi - 2) if gi >= 2 else ((tc - 1, gi + 6) if tc > 0 else None)
                if pv is not None:
                    g.w(pe, "se", red[pv])
                for ct in range(CT):
                    mm = pe.matmul(pb[2 + gi % 2][:],
                                   xxnT[ct][:, tc * 512 + sub * 128:tc * 512 + (sub + 1) * 128],
                                   WrcS[:, ct * 1024 + dc * 512:ct * 1024 + (dc + 1) * 512],
                                   start=(ct == 0), stop=(ct == CT - 1))
                rmm[(tc, gi)] = g.i(mm, "pe")

            @g.block.scalar
            def _(se, tc=tc, gi=gi, sub=sub, dc=dc):
                g.w(se, "pe", rmm[(tc, gi)])
                if tc > 0:
                    g.w(se, "ve", cmbd[(tc - 1, sub)])
                red[(tc, gi)] = g.i(
                    se.activation(rcm[:, sub * 1024 + dc * 512:sub * 1024 + (dc + 1) * 512],
                                  pb[2 + gi % 2][:], AF.Sigmoid), "se")

        if tc == 0:
            @g.block.gpsimd
            def _(gp):
                kyd[(0, 0)] = load_wky(gp, 0, 0)
                kyd[(0, 1)] = load_wky(gp, 1, 1)

        for ft in range(32):
            fg, fi = ft // 4, ft % 4

            @g.block.tensor
            def _(pe, tc=tc, ft=ft, fg=fg, fi=fi):
                if fi == 0:
                    g.w(pe, "gp", kyd[(tc, fg)])
                pv = (tc, ft - 2) if ft >= 2 else ((tc - 1, ft + 30) if tc > 0 else None)
                if pv is not None:
                    g.w(pe, "se", krl[pv])
                if ft == 0 and tc > 0:
                    g.w(pe, "ve", rvd[tc - 1])
                for ct in range(CT):
                    mm = pe.matmul(pb[ft % 2][:],
                                   wky[fg % 2][:, ct * 512 + fi * 128:ct * 512 + (fi + 1) * 128],
                                   xxnT[ct][:, tc * 512:(tc + 1) * 512],
                                   start=(ct == 0), stop=(ct == CT - 1))
                kmm[(tc, ft)] = g.i(mm, "pe")

            if fi == 3:
                if fg + 2 < 8:
                    @g.block.gpsimd
                    def _(gp, tc=tc, fg=fg):
                        g.w(gp, "pe", kmm[(tc, fg * 4 + 3)])
                        kyd[(tc, fg + 2)] = load_wky(gp, fg + 2, fg % 2)
                elif tc + 1 < NTC:
                    @g.block.gpsimd
                    def _(gp, tc=tc, fg=fg):
                        g.w(gp, "pe", kmm[(tc, fg * 4 + 3)])
                        kyd[(tc + 1, fg % 2)] = load_wky(gp, fg % 2, fg % 2)

            @g.block.scalar
            def _(se, tc=tc, ft=ft):
                g.w(se, "pe", kmm[(tc, ft)])
                pv = (tc, ft - 2) if ft >= 2 else ((tc - 1, ft + 30) if tc > 0 else None)
                if pv is not None:
                    g.w(se, "ve", kbf[pv])
                krl[(tc, ft)] = g.i(
                    se.activation(rl[:, (ft % 2) * 512:(ft % 2 + 1) * 512],
                                  pb[ft % 2][:], AF.Relu), "se")

            @g.block.vector
            def _(ve, tc=tc, ft=ft):
                g.w(ve, "se", krl[(tc, ft)])
                if ft == 0 and tc > 0:
                    g.w(ve, "pe", vmm[(tc - 1, 7)])
                kbf[(tc, ft)] = g.i(
                    ve.tensor_mul(kcm[ft][:],
                                  rl[:, (ft % 2) * 512:(ft % 2 + 1) * 512],
                                  rl[:, (ft % 2) * 512:(ft % 2 + 1) * 512]), "ve")

        if tc == 0:
            @g.block.gpsimd
            def _(gp):
                vld[(0, 0)] = load_wvl(gp, 0, 0)
                vld[(0, 1)] = load_wvl(gp, 1, 1)

        for fg in range(8):
            @g.block.tensor
            def _(pe, tc=tc, fg=fg):
                g.w(pe, "gp", vld[(tc, fg)])
                if fg == 0:
                    g.w(pe, "se", krl[(tc, 31)])
                    g.w(pe, "se", red[(tc, 7)])
                    g.w(pe, "ve", kbf[(tc, 31)])
                for fi in range(4):
                    ft = fg * 4 + fi
                    for j in range(8):
                        sub, dc = j // 2, j % 2
                        mm = pe.matmul(pb[j][:],
                                       kcm[ft][:, sub * 128:(sub + 1) * 128],
                                       wvl[fg % 2][:, fi * 1024 + dc * 512:fi * 1024 + (dc + 1) * 512],
                                       start=(ft == 0), stop=(ft == 31))
                vmm[(tc, fg)] = g.i(mm, "pe")

            if fg + 2 < 8:
                @g.block.gpsimd
                def _(gp, tc=tc, fg=fg):
                    g.w(gp, "pe", vmm[(tc, fg)])
                    vld[(tc, fg + 2)] = load_wvl(gp, fg + 2, fg % 2)
            elif tc + 1 < NTC:
                @g.block.gpsimd
                def _(gp, tc=tc, fg=fg):
                    g.w(gp, "pe", vmm[(tc, fg)])
                    vld[(tc + 1, fg % 2)] = load_wvl(gp, fg % 2, fg % 2)

        @g.block.vector
        def _(ve, tc=tc):
            g.w(ve, "pe", vmm[(tc, 7)])
            for sub in range(4):
                for dc in range(2):
                    j = sub * 2 + dc
                    sl = rcm[:, sub * 1024 + dc * 512:sub * 1024 + (dc + 1) * 512]
                    mm = ve.tensor_mul(sl, pb[j][:], sl)
            rvd[tc] = g.i(mm, "ve")

        for sub in range(4):
            tt = tc * 4 + sub
            rot = sub % 2

            @g.block.sync
            def _(sy, tc=tc, sub=sub, tt=tt, rot=rot):
                if sub >= 2 or tc > 0:
                    p = (tc, sub - 2) if sub >= 2 else (tc - 1, sub + 2)
                    g.w(sy, "ve", cmbd[p])
                    g.w(sy, "se", sild[p])
                    g.w(sy, "sy", sto[p])
                ldd[(tt, "xx")] = g.dma(sy, "sy", xxt2[rot][:],
                                        E["xx_h"][tt * 128:(tt + 1) * 128, :])
                ldd[(tt, "a")] = g.dma(sy, "sy", a1t[rot][:],
                                       E["acc1_h"][tt * 128:(tt + 1) * 128, :])
                ldd[(tt, "x1")] = g.dma(sy, "sy", x1t[rot][:],
                                        E["x1_h"][tt * 128:(tt + 1) * 128, :])

            @g.block.scalar
            def _(se, tc=tc, sub=sub, tt=tt, rot=rot):
                g.w(se, "sy", ldd[(tt, "a")])
                if sub >= 2 or tc > 0:
                    p = (tc, sub - 2) if sub >= 2 else (tc - 1, sub + 2)
                    g.w(se, "ve", cmbd[p])
                sild[(tc, sub)] = g.i(se.activation(g1t[rot][:], a1t[rot][:], AF.Silu), "se")

            @g.block.vector
            def _(ve, tc=tc, sub=sub, tt=tt, rot=rot):
                g.w(ve, "se", sild[(tc, sub)])
                g.w(ve, "sy", ldd[(tt, "x1")])
                ve.tensor_add(xxt2[rot][:], xxt2[rot][:],
                              rcm[:, sub * 1024:(sub + 1) * 1024])
                ve.tensor_mul(a1t[rot][:], xxt2[rot][:], g1t[rot][:])
                cmbd[(tc, sub)] = g.i(ve.tensor_add(a1t[rot][:], a1t[rot][:], x1t[rot][:]), "ve")

            @g.block.sync
            def _(sy, tc=tc, sub=sub, tt=tt, rot=rot):
                g.w(sy, "ve", cmbd[(tc, sub)])
                sto[(tc, sub)] = g.dma(sy, "sy", E["out_h"][tt * 128:(tt + 1) * 128, :],
                                       a1t[rot][:])

    st3.close()
    st.close()



def _prep_shared(inputs):
    f32 = np.float32

    def t(a):
        return np.ascontiguousarray(np.asarray(a, dtype=f32))

    def tr(a):
        return np.ascontiguousarray(np.asarray(a, dtype=f32).T)

    def col8(a):
        return np.ascontiguousarray(np.asarray(a, dtype=f32).reshape(CT, 128).T)
    col8_ref = [col8]

    cst = np.zeros((128, 8), f32)
    cst[:, I_ONE] = 1.0
    cst[:, I_NEG1] = -1.0
    cst[:, I_EPS] = LN_EPS
    cst[:, I_E6] = 1e-6
    cst[:, I_E8] = 1e-8
    smal = np.concatenate([
        cst,
        col8_ref[0](inputs["L0_attn_time_decay"]),
        col8_ref[0](inputs["L0_attn_time_first"]),
        col8_ref[0](inputs["L1_attn_time_decay"]),
        col8_ref[0](inputs["L1_attn_ln1_g"]),
        col8_ref[0](inputs["L1_attn_ln1_b"]),
        col8_ref[0](inputs["L1_attn_ln2_g"]),
        col8_ref[0](inputs["L1_attn_ln2_b"]),
    ], axis=1)

    return {
        "WvT": tr(inputs["L0_attn_value_w"]),
        "WrT": tr(inputs["L0_attn_recep_w"]),
        "WoT": tr(inputs["L0_attn_out_w"]),
        "gw": t(inputs["L0_ffn_gate_w"]),
        "w1": t(inputs["L0_ffn_w1"]),
        "w3": t(inputs["L0_ffn_w3"]),
        "w2": t(inputs["L0_ffn_w2"]),
        "Wr1": tr(inputs["L1_attn_Wr"]),
        "Ww1": tr(inputs["L1_attn_Ww"]),
        "Wk1": tr(inputs["L1_attn_Wk"]),
        "Wv1": tr(inputs["L1_attn_Wv"]),
        "Wo1": tr(inputs["L1_attn_Wo"]),
        "Wky": tr(inputs["L1_attn_Wkey"]),
        "Wvl": tr(inputs["L1_attn_Wval"]),
        "Wrc": tr(inputs["L1_attn_Wrec"]),
        "rw": tr(inputs["L1_ffn_router_w"]),
        "rb": t(inputs["L1_ffn_router_b"]).reshape(1, NE),
        "ew": np.ascontiguousarray(
            np.asarray(inputs["L1_ffn_expert_w"], dtype=f32).transpose(0, 2, 1)),
        "ebv": t(inputs["L1_ffn_expert_b"]),
        "smal": smal,
        "idn": np.eye(128, dtype=f32),
    }


def kernel(**inputs):
    x = np.asarray(inputs["L0_attn_x"], dtype=np.float32)
    shared = _prep_shared(inputs)
    nc = build_program()
    in_maps = []
    for b in range(8):
        m = dict(shared)
        m["x"] = np.ascontiguousarray(x[b])
        m["xT"] = np.ascontiguousarray(x[b].T)
        in_maps.append(m)
    res = run_bass_kernel_spmd(nc, in_maps, core_ids=list(range(8)))
    return np.stack([r["out"] for r in res.results], axis=0)


# version 24
# speedup vs baseline: 1.0467x; 1.0467x over previous
import numpy as np
from contextlib import ExitStack

import concourse.bass as bass
import concourse.mybir as mybir
from concourse.bass_utils import run_bass_kernel_spmd

AF = mybir.ActivationFunctionType
ALU = mybir.AluOpType
AX = mybir.AxisListType.X
F32 = mybir.dt.float32
BF16 = mybir.dt.bfloat16

T = 2048
C = 1024
FF = 4096
NE = 8
CT = C // 128
TT = T // 128
NTC = T // 512
LN_EPS = 1e-5
NSLAB = 4

I_ONE, I_NEG1, I_EPS, I_E6, I_E8 = 0, 1, 2, 3, 4


class G:

    def __init__(self, nc, block):
        self.nc = nc
        self.block = block
        self.sem = {}
        self.c = {}
        self._cms = []
        for name in ("gp", "sy", "pe", "ve", "se", "pc", "sq", "vq"):
            cm = nc.semaphore("sem_" + name)
            self._cms.append(cm)
            self.sem[name] = cm.__enter__()
            self.c[name] = 0

    def i(self, inst, s, n=1):
        inst.then_inc(self.sem[s], n)
        self.c[s] += n
        return self.c[s]

    def dma(self, eng, q, out, in_, accum=None, transpose=False):
        eng.wait_ge(self.sem[q], self.c[q])
        if transpose:
            inst = eng.dma_start_transpose(out, in_)
        elif accum is not None:
            inst = eng.dma_start(out=out, in_=in_, accum_op=accum)
        else:
            inst = eng.dma_start(out=out, in_=in_)
        return self.i(inst, q, 16)

    def w(self, eng, s, v):
        if v and v > 0:
            eng.wait_ge(self.sem[s], v)

    def barrier(self):
        for name, sec in (("pe", self.block.tensor), ("ve", self.block.vector),
                          ("se", self.block.scalar)):
            @sec
            def _(e, name=name):
                e.drain()
                self.i(e.nop(), name)
        tgt = dict(self.c)
        for sec in (self.block.tensor, self.block.vector, self.block.scalar,
                    self.block.sync, self.block.gpsimd):
            @sec
            def _(e):
                for s in ("gp", "sy", "pe", "ve", "se", "pc", "sq", "vq"):
                    self.w(e, s, tgt[s])


def _r3(ap, a, inner):
    return ap.rearrange("p (a n) -> p a n", a=a)



def build_program(upto=None):
    nc = bass.Bass("TRN2", target_bir_lowering=False)
    dtn = nc.dram_tensor
    E = {}

    E["upto"] = upto
    for nm, shp in (("x", [T, C]), ("xT", [C, T]), ("WvT", [C, C]),
                    ("WrT", [C, C]), ("WoT", [C, C]), ("gw", [C, NE]),
                    ("w1", [NE, C, FF]), ("w3", [NE, C, FF]), ("w2", [NE, FF, C]),
                    ("Wr1", [C, C]), ("Ww1", [C, C]), ("Wk1", [C, C]),
                    ("Wv1", [C, C]), ("Wo1", [C, C]),
                    ("Wky", [C, FF]), ("Wvl", [FF, C]), ("Wrc", [C, C]),
                    ("rw", [C, NE]), ("rb", [1, NE]), ("ew", [NE, C, C]),
                    ("ebv", [NE, C]), ("smal", [128, 64]), ("idn", [128, 128])):
        E[nm + "_h"] = dtn(nm, shp, F32, kind="ExternalInput").ap()

    for nm in ("m0r", "r0r", "m1r", "r1r", "m2r", "r2r", "m3r", "r3r", "p4r", "q4r"):
        E[nm + "_h"] = dtn(nm, [1, T], F32, kind="Internal").ap()
    for nm in ("acc0", "x1", "acc1", "xx"):
        E[nm + "_h"] = dtn(nm, [T, C], F32, kind="Internal").ap()
    E["x1b_h"] = dtn("x1b", [T, C], BF16, kind="Internal").ap()
    E["x1r_h"] = dtn("x1r", [T, C], BF16, kind="Internal").ap()
    E["xxb_h"] = dtn("xxb", [T, C], BF16, kind="Internal").ap()
    E["out_h"] = dtn("out", [T, C], F32, kind="ExternalOutput").ap()

    E["dbg"] = {}
    if upto is not None:
        for nm, shp, dt_ in (("xnT", [CT, 128, T], BF16), ("wi", [128, TT * NE], F32),
                             ("acc0", [T, C], F32), ("rwkv", [CT, 128, T], BF16),
                             ("x1", [T, C], F32), ("xnT1", [CT, 128, T], BF16),
                             ("acc1", [T, C], F32), ("rwkv1", [CT, 128, T], BF16),
                             ("xx", [T, C], F32), ("mw", [128, TT * NE], F32)):
            E["dbg"][nm] = dtn("dbg_" + nm, shp, dt_, kind="ExternalOutput").ap()

    al = nc.alloc_sbuf_tensor
    E["smal"] = al("smalS", [128, 64], F32)
    sm = E["smal"]
    E["cst"] = sm[:, 0:8]
    for i, nm in enumerate(("td0c", "tf0c", "td1c", "g1c", "b1c", "g2c", "b2c")):
        E[nm] = sm[:, 8 * (i + 1):8 * (i + 2)]
    E["idn"] = al("idnS", [128, 128], F32)
    E["gwb"] = al("gwb", [128, CT * NE], BF16)
    E["rwb"] = al("rwb", [128, CT * NE], BF16)
    E["ebb"] = al("ebb", [NE, C], BF16)
    E["rbbc"] = al("rbbc", [128, NE], F32)
    E["wiA"] = al("wiA", [128, TT * NE], F32)
    E["mwA"] = al("mwA", [128, TT * NE], F32)
    E["m1c"] = al("m1c", [128, TT], F32)
    E["r1c"] = al("r1c", [128, TT], F32)
    E["v1q"] = al("v1q", [128, TT], F32)
    E["Wo1s"] = al("Wo1sP", [128, CT * 1024], BF16)
    E["csc"] = al("cscS", [128, 32], F32)
    E["sc2"] = al("sc2S", [128, TT], F32)
    E["sc3"] = al("sc3S", [128, TT], F32)
    E["trs"] = al("trsS", [16, 128], F32)

    E["pb"] = [nc.alloc_psum_tensor(f"pb{i}", [128, 512], F32) for i in range(8)]

    with nc.Block() as block:
        g = G(nc, block)
        _emit(g, E)

    return nc


def _emit(g, E):
    nc = g.nc
    upto = E["upto"]

    _p0_preload(g, E)
    g.barrier()

    l0st = ExitStack()
    E["xnT"] = [l0st.enter_context(nc.sbuf_tensor(f"xnT{i}", [128, T], BF16))
                for i in range(CT)]
    _p12_ln0(g, E)
    g.barrier()
    if upto == "P2":
        _dump_fm(g, E, "xnT", E["xnT"]); _finish(g, E); return

    _p3_gate(g, E)
    g.barrier()
    E["xn32_st"].close()
    if upto == "P3":
        _dump_plain(g, E, "wi", E["wiA"][:]); _finish(g, E); return

    _p4_moe_l0(g, E)
    g.barrier()
    if upto == "P4":
        _dump_hbm(g, E, "acc0", E["acc0_h"]); _finish(g, E); return

    E["rwkv"] = [l0st.enter_context(nc.sbuf_tensor(f"rwkv{i}", [128, T], BF16))
                 for i in range(CT)]
    _p5_attn_l0(g, E)
    g.barrier()
    if upto == "P5":
        _dump_fm(g, E, "rwkv", E["rwkv"]); _finish(g, E); return

    _p6_combine_l0(g, E)
    g.barrier()
    if upto == "P6":
        _dump_hbm(g, E, "x1", E["x1_h"]); _finish(g, E); return
    l0st.close()

    q3st = ExitStack()
    E["xn1T"] = [q3st.enter_context(nc.sbuf_tensor(f"xn1Tq3_{i}", [128, T], BF16))
                 for i in range(CT)]
    l1st = ExitStack()
    E["xnT1"] = [l1st.enter_context(nc.sbuf_tensor(f"xnT1_{i}", [128, T], BF16))
                 for i in range(CT)]
    _q12_l1(g, E, l1st, upto)
    g.barrier()
    if upto == "Q1":
        _dump_fm(g, E, "xnT1", E["xnT1"]); _finish(g, E); return
    if upto == "Q2":
        _dump_hbm(g, E, "acc1", E["acc1_h"])
        _dump_plain(g, E, "mw", E["mwA"][:])
        _finish(g, E); return
    l1st.close()

    l1bst = ExitStack()
    E["rwkv1"] = [l1bst.enter_context(nc.sbuf_tensor(f"rwkv1_{i}", [128, T], BF16))
                  for i in range(CT)]
    _q3_timemix_l1(g, E)
    g.barrier()
    if upto == "Q3":
        _dump_fm(g, E, "rwkv1", E["rwkv1"]); _finish(g, E); return


    _q4a_xx(g, E)
    g.barrier()
    l1bst.close()
    q3st.close()
    if upto == "Q4A":
        _dump_hbm(g, E, "xx", E["xx_h"]); _finish(g, E); return

    _q4b_final(g, E)
    _finish(g, E)



def _dump_fm(g, E, key, tiles):
    out = E["dbg"][key]

    @g.block.sync
    def _(sy):
        for ct, tl in enumerate(tiles):
            g.dma(sy, "sy", out[ct], tl[:])


def _dump_plain(g, E, key, ap):
    out = E["dbg"][key]

    @g.block.sync
    def _(sy):
        g.dma(sy, "sy", out, ap)


def _dump_hbm(g, E, key, hbm_ap):
    out = E["dbg"][key]

    @g.block.sync
    def _(sy):
        g.dma(sy, "sy", out, hbm_ap)


def _finish(g, E):
    @g.block.sync
    def _(sy):
        g.w(sy, "gp", g.c["gp"])
        g.w(sy, "sy", g.c["sy"])
        g.w(sy, "sq", g.c["sq"])
        g.w(sy, "vq", g.c["vq"])



def _p0_preload(g, E):
    @g.block.sync
    def _(sy):
        g.dma(sy, "sy", E["smal"][:], E["smal_h"])
        g.dma(sy, "sy", E["idn"][:], E["idn_h"])
        g.dma(sy, "sy", E["rbbc"][:], E["rb_h"].broadcast_to([128, NE]))

    @g.block.gpsimd
    def _(gp):
        g.dma(gp, "gp", _r3(E["gwb"][:], CT, NE),
              E["gw_h"].rearrange("(a p) e -> p a e", p=128))
        g.dma(gp, "gp", _r3(E["rwb"][:], CT, NE),
              E["rw_h"].rearrange("(a p) e -> p a e", p=128))
        g.dma(gp, "gp", E["ebb"][:], E["ebv_h"])



def _p12_ln0(g, E):
    nc = g.nc
    pb = E["pb"]
    cst = E["cst"]
    ones = cst[:, I_ONE:I_ONE + 1]
    neg1 = cst[0:1, I_NEG1:I_NEG1 + 1]
    st = ExitStack()
    sb = nc.sbuf_tensor
    E["xn32_st"] = ExitStack()
    xTs = [E["xn32_st"].enter_context(sb(f"xT{i}", [128, T], F32)) for i in range(CT)]
    E["xn32"] = xTs
    xq = st.enter_context(sb("xqA", [128, T], F32))
    row0 = st.enter_context(sb("row0A", [1, T], F32))
    row1 = st.enter_context(sb("row1A", [1, T], F32))
    row2 = st.enter_context(sb("row2A", [1, T], F32))
    m0bc = st.enter_context(sb("m0bcA", [128, T], F32))
    r0bc = st.enter_context(sb("r0bcA", [128, T], F32))
    xnT = E["xnT"]

    ld_ct = [0] * CT
    ld_last = [0]

    @g.block.sync
    def _(sy):
        for ct in range(CT):
            ld_ct[ct] = g.dma(sy, "sy", xTs[ct][:],
                              E["xT_h"][ct * 128:(ct + 1) * 128, :])
        ld_last[0] = ld_ct[CT - 1]

    summ = {}

    @g.block.tensor
    def _(pe):
        for ct in range(CT):
            g.w(pe, "sy", ld_ct[ct])
            for tc in range(NTC):
                mm = pe.matmul(pb[tc][0:1, :], ones,
                               xTs[ct][:, tc * 512:(tc + 1) * 512],
                               start=(ct == 0), stop=(ct == CT - 1))
        summ[0] = g.i(mm, "pe")

    sqd = {}
    sqm = {}
    for ct in range(CT):
        @g.block.scalar
        def _(se, ct=ct):
            g.w(se, "sy", ld_ct[ct])
            if ct >= 1:
                g.w(se, "pe", sqm[ct - 1])
            sqd[ct] = g.i(se.activation(xq[:], xTs[ct][:], AF.Square), "se")

        @g.block.tensor
        def _(pe, ct=ct):
            g.w(pe, "se", sqd[ct])
            for tc in range(NTC):
                mm = pe.matmul(pb[4 + tc][0:1, :], ones,
                               xq[:, tc * 512:(tc + 1) * 512],
                               start=(ct == 0), stop=(ct == CT - 1))
            sqm[ct] = g.i(mm, "pe")

    rows_done = [0]
    row0_done = [0]

    @g.block.vector
    def _(ve):
        g.w(ve, "pe", sqm[CT - 1])
        g.w(ve, "pe", summ[0])
        for tc in range(NTC):
            ve.tensor_copy(row0[0:1, tc * 512:(tc + 1) * 512], pb[tc][0:1, :])
            ve.tensor_copy(row1[0:1, tc * 512:(tc + 1) * 512], pb[4 + tc][0:1, :])
        row0_done[0] = g.i(ve.tensor_scalar_mul(row0[:], row0[:], 1.0 / C), "ve")
        ve.tensor_scalar_mul(row1[:], row1[:], 1.0 / C)
        ve.drain()
        ve.tensor_mul(row2[:], row0[:], row0[:])
        ve.drain()
        rows_done[0] = g.i(ve.tensor_sub(row1[:], row1[:], row2[:]), "ve")

    sd_done = [0]

    @g.block.scalar
    def _(se):
        g.w(se, "ve", rows_done[0])
        se.activation(row1[:], row1[:], AF.Sqrt, bias=cst[0:1, I_EPS:I_EPS + 1])
        se.drain()
        se.activation(row2[:], row1[:], AF.Ln)
        se.drain()
        sd_done[0] = g.i(se.activation(row1[:], row2[:], AF.Exp, scale=neg1), "se")

    bc_done = [0]

    @g.block.sync
    def _(sy):
        g.w(sy, "se", sd_done[0])
        g.w(sy, "ve", rows_done[0])
        g.w(sy, "ve", row0_done[0])
        g.dma(sy, "sy", E["m0r_h"], row0[:])
        g.dma(sy, "sy", E["r0r_h"], row1[:])
        g.dma(sy, "sy", m0bc[:], E["m0r_h"].broadcast_to([128, T]))
        bc_done[0] = g.dma(sy, "sy", r0bc[:], E["r0r_h"].broadcast_to([128, T]))

    @g.block.vector
    def _(ve):
        g.w(ve, "sy", bc_done[0])
        for ct in range(CT):
            ve.tensor_sub(xTs[ct][:], xTs[ct][:], m0bc[:])
            ve.tensor_mul(xTs[ct][:], xTs[ct][:], r0bc[:])
            g.i(ve.tensor_copy(xnT[ct][:], xTs[ct][:]), "ve")

    g.barrier()
    st.close()



def _p3_gate(g, E):
    nc = g.nc
    pb = E["pb"]
    xnT = E["xnT"]
    wiA = E["wiA"]
    csc = E["csc"]
    st = ExitStack()
    lgt = st.enter_context(nc.sbuf_tensor("lgtP3", [128, 2 * NE], F32))
    l2t = st.enter_context(nc.sbuf_tensor("l2tP3", [128, 2 * NE], F32))
    gwf = st.enter_context(nc.sbuf_tensor("gwfP3", [128, CT * NE], F32))
    xn32 = E["xn32"]

    gld = [0]

    @g.block.sync
    def _(sy):
        gld[0] = g.dma(sy, "sy", _r3(gwf[:], CT, NE),
                       E["gw_h"].rearrange("(a p) e -> p a e", p=128))

    lgd = {}
    fin = {}
    mid = {}
    for tt in range(TT):
        bank = 6 + (tt % 2)

        @g.block.tensor
        def _(pe, tt=tt, bank=bank):
            if tt >= 2:
                g.w(pe, "ve", fin[tt - 2])
            g.w(pe, "sy", gld[0])
            for ct in range(CT):
                mm = pe.matmul(pb[bank][:, 0:NE],
                               xn32[ct][:, tt * 128:(tt + 1) * 128],
                               gwf[:, ct * NE:(ct + 1) * NE],
                               start=(ct == 0), stop=(ct == CT - 1))
            lgd[tt] = g.i(mm, "pe")

        @g.block.vector
        def _(ve, tt=tt, bank=bank):
            sl = slice((tt % 2) * NE, (tt % 2) * NE + NE)
            lg, l2 = lgt[:, sl], l2t[:, sl]
            g.w(ve, "pe", lgd[tt])
            ve.tensor_copy(lg, pb[bank][:, 0:NE])
            ve.drain()
            ve.reduce_max(csc[:, 0:1], lg, axis=AX)
            ve.drain()
            ve.tensor_single_scalar(l2, lg, csc[:, 0:1], ALU.is_lt)
            ve.drain()
            ve.tensor_scalar(l2, l2, 1.0, 1e30, ALU.subtract, ALU.mult)
            ve.drain()
            ve.tensor_add(l2, l2, lg)
            ve.drain()
            ve.reduce_max(csc[:, 1:2], l2, axis=AX)
            ve.drain()
            mid[tt] = g.i(ve.tensor_sub(csc[:, 2:3], csc[:, 1:2], csc[:, 0:1]), "ve")

        @g.block.scalar
        def _(se, tt=tt):
            g.w(se, "ve", mid[tt])
            mid[("e", tt)] = g.i(se.activation(csc[:, 3:4], csc[:, 2:3], AF.Exp), "se")

        @g.block.vector
        def _(ve, tt=tt):
            sl = slice((tt % 2) * NE, (tt % 2) * NE + NE)
            lg, l2 = lgt[:, sl], l2t[:, sl]
            g.w(ve, "se", mid[("e", tt)])
            ve.tensor_scalar_add(csc[:, 4:5], csc[:, 3:4], 1.0)
            ve.drain()
            ve.reciprocal(csc[:, 5:6], csc[:, 4:5])
            ve.drain()
            ve.tensor_mul(csc[:, 6:7], csc[:, 3:4], csc[:, 5:6])
            ve.drain()
            wsl = wiA[:, tt * NE:(tt + 1) * NE]
            ve.tensor_scalar(wsl, lg, csc[:, 0:1], csc[:, 5:6], ALU.is_equal, ALU.mult)
            ve.drain()
            ve.tensor_scalar(l2, l2, csc[:, 1:2], csc[:, 6:7], ALU.is_equal, ALU.mult)
            ve.drain()
            fin[tt] = g.i(ve.tensor_add(wsl, wsl, l2), "ve")

    g.barrier()
    st.close()



def _p4_moe_l0(g, E):
    nc = g.nc
    pb = E["pb"]
    xnT = E["xnT"]
    wiA = E["wiA"]
    st = ExitStack()
    sb = nc.sbuf_tensor
    h = [st.enter_context(sb(f"hP4_{i}", [128, T], BF16)) for i in range(8)]
    w1s = [st.enter_context(sb(f"w1sP4_{i}", [128, CT * 1024], BF16)) for i in range(2)]
    w3s = [st.enter_context(sb(f"w3sP4_{i}", [128, CT * 1024], BF16)) for i in range(2)]
    w2s = st.enter_context(sb("w2sP4", [128, 8 * 1024], BF16))
    sil = st.enter_context(sb("silP4", [128, 2 * 512], F32))
    stg = [st.enter_context(sb(f"stgP4_{i}", [128, 2048], F32)) for i in range(2)]

    NS = NE * NSLAB

    def load_w13(gp, dst1, dst3, e, s):
        c0 = s * 1024
        g.dma(gp, "gp", _r3(dst1[:], CT, 1024),
              E["w1_h"][e, :, c0:c0 + 1024].rearrange("(a p) f -> p a f", p=128))
        return g.dma(gp, "gp", _r3(dst3[:], CT, 1024),
                     E["w3_h"][e, :, c0:c0 + 1024].rearrange("(a p) f -> p a f", p=128))

    def load_w2(gp, e, s):
        f0 = s * 1024
        return g.dma(gp, "gp", _r3(w2s[:], 8, 1024),
                     E["w2_h"][e, f0:f0 + 1024, :].rearrange("(a p) d -> p a d", p=128))

    w13_init = [0, 0]

    @g.block.gpsimd
    def _(gp):
        load_w13(gp, w1s[0], w3s[0], 0, 0)
        w13_init[0] = g.c["gp"]
        load_w13(gp, w1s[1], w3s[1], 0, 1)
        w13_init[1] = g.c["gp"]

    bp = dict(g.c)

    def sil_at(kg):
        return bp["se"] + kg + 1

    def hbf_at(si, kl):
        return bp["ve"] + si * 48 + kl + 1

    def yev_at(si, q):
        return bp["ve"] + si * 48 + 32 + q + 1

    def h1_at(si, kl):
        return bp["pe"] + si * 96 + 2 * kl + 1

    def h3_at(si, kl):
        return bp["pe"] + si * 96 + 2 * kl + 2

    def ymm_at(si, q, j):
        return bp["pe"] + si * 96 + 64 + q * 2 + j + 1

    def yend_at(si):
        return bp["pe"] + si * 96 + 96

    def gp_slab_dmas(si):
        return 11 if si + 2 < NS else 9

    def gp_pre(si):
        return bp["gp"] + 16 * sum(gp_slab_dmas(s2) for s2 in range(si))

    def w2_at(si):
        return gp_pre(si) + 16

    def w13_at(si):
        if si == 0:
            return w13_init[0]
        if si == 1:
            return w13_init[1]
        return gp_pre(si - 2) + 16 + 128 + 32

    def acc_at(si, rl):
        return gp_pre(si) + 16 + (rl + 1) * 16

    slabs = [(e, s) for e in range(NE) for s in range(NSLAB)]
    for si, (e, s) in enumerate(slabs):
        buf = s % 2

        @g.block.gpsimd
        def _(gp, e=e, s=s, si=si):
            if si >= 1:
                g.w(gp, "pe", yend_at(si - 1))
            got = load_w2(gp, e, s)
            assert got == w2_at(si), (si, got, w2_at(si))

        @g.block.tensor
        def _(pe, e=e, s=s, si=si, buf=buf):
            g.w(pe, "gp", w13_at(si))
            for ft in range(8):
                for tc in range(NTC):
                    kl = ft * 4 + tc
                    kg = si * 32 + kl
                    if kg >= 2:
                        g.w(pe, "se", sil_at(kg - 2))
                        pv = (si, kl - 2) if kl >= 2 else (si - 1, kl + 30)
                        g.w(pe, "ve", hbf_at(*pv))
                    for ct in range(CT):
                        mm = pe.matmul(pb[tc % 2][:],
                                       w1s[buf][:, ct * 1024 + ft * 128:ct * 1024 + (ft + 1) * 128],
                                       xnT[ct][:, tc * 512:(tc + 1) * 512],
                                       start=(ct == 0), stop=(ct == CT - 1))
                    assert g.i(mm, "pe") == h1_at(si, kl)
                    for ct in range(CT):
                        mm = pe.matmul(pb[2 + tc % 2][:],
                                       w3s[buf][:, ct * 1024 + ft * 128:ct * 1024 + (ft + 1) * 128],
                                       xnT[ct][:, tc * 512:(tc + 1) * 512],
                                       start=(ct == 0), stop=(ct == CT - 1))
                    assert g.i(mm, "pe") == h3_at(si, kl)

        @g.block.scalar
        def _(se, si=si):
            for ft in range(8):
                for tc in range(NTC):
                    kl = ft * 4 + tc
                    kg = si * 32 + kl
                    g.w(se, "pe", h1_at(si, kl))
                    if kg >= 2:
                        pv = (si, kl - 2) if kl >= 2 else (si - 1, kl + 30)
                        g.w(se, "ve", hbf_at(*pv))
                    got = g.i(se.activation(sil[:, (kg % 2) * 512:(kg % 2 + 1) * 512],
                                            pb[tc % 2][:], AF.Silu), "se")
                    assert got == sil_at(kg)

        @g.block.vector
        def _(ve, si=si):
            for ft in range(8):
                for tc in range(NTC):
                    kl = ft * 4 + tc
                    kg = si * 32 + kl
                    g.w(ve, "se", sil_at(kg))
                    g.w(ve, "pe", h3_at(si, kl))
                    if kl == 0 and si >= 1:
                        g.w(ve, "pe", yend_at(si - 1))
                    got = g.i(ve.tensor_mul(h[ft][:, tc * 512:(tc + 1) * 512],
                                            sil[:, (kg % 2) * 512:(kg % 2 + 1) * 512],
                                            pb[2 + tc % 2][:]), "ve")
                    assert got == hbf_at(si, kl)

        @g.block.tensor
        def _(pe, si=si):
            g.w(pe, "gp", w2_at(si))
            for tc in range(NTC):
                g.w(pe, "ve", hbf_at(si, 28 + tc))
                for sub in range(4):
                    q = tc * 4 + sub
                    if q >= 2:
                        g.w(pe, "ve", yev_at(si, q - 2))
                    elif si >= 1:
                        g.w(pe, "ve", yev_at(si - 1, 14 + q))
                    for dc in range(2):
                        bank = 4 + 2 * (q % 2) + dc
                        for ft in range(8):
                            mm = pe.matmul(
                                pb[bank][:],
                                h[ft][:, tc * 512 + sub * 128:tc * 512 + (sub + 1) * 128],
                                w2s[:, ft * 1024 + dc * 512:ft * 1024 + (dc + 1) * 512],
                                start=(ft == 0), stop=(ft == 7))
                        assert g.i(mm, "pe") == ymm_at(si, q, dc)
            assert g.c["pe"] == yend_at(si)

        @g.block.vector
        def _(ve, e=e, si=si):
            for tc in range(NTC):
                for sub in range(4):
                    q = tc * 4 + sub
                    ridx = si * 8 + tc * 2 + sub // 2
                    g.w(ve, "pe", ymm_at(si, q, 1))
                    if sub % 2 == 0 and ridx >= 2:
                        g.w(ve, "gp", acc_at((ridx - 2) // 8, (ridx - 2) % 8))
                    tt = tc * 4 + sub
                    wcol = wiA[:, tt * NE + e:tt * NE + e + 1]
                    for dc in range(2):
                        bank = 4 + 2 * (q % 2) + dc
                        j = (sub % 2) * 2 + dc
                        mm = ve.tensor_single_scalar(
                            stg[ridx % 2][:, j * 512:(j + 1) * 512],
                            pb[bank][:], wcol, ALU.mult)
                    assert g.i(mm, "ve") == yev_at(si, q)

        @g.block.gpsimd
        def _(gp, si=si):
            for tc in range(NTC):
                for r in range(2):
                    rl = tc * 2 + r
                    ridx = si * 8 + rl
                    g.w(gp, "ve", yev_at(si, tc * 4 + r * 2 + 1))
                    q0 = tc * 512 + r * 256
                    got = g.dma(
                        gp, "gp",
                        E["acc0_h"][q0:q0 + 256, :].rearrange(
                            "(s p) (d n) -> p s d n", p=128, n=512),
                        stg[ridx % 2][:].rearrange("p (s d n) -> p s d n", s=2, d=2),
                        accum=None if si == 0 else ALU.add)
                    assert got == acc_at(si, rl), (si, rl, got, acc_at(si, rl))

        if si + 2 < NS:
            @g.block.gpsimd
            def _(gp, si=si, buf=buf):
                g.w(gp, "pe", h3_at(si, 31))
                ne2, ns2 = slabs[si + 2]
                got = load_w13(gp, w1s[buf], w3s[buf], ne2, ns2)
                assert got == w13_at(si + 2), (si, got)

    st.close()



def _p5_attn_l0(g, E):
    nc = g.nc
    pb = E["pb"]
    xnT = E["xnT"]
    rwkv = E["rwkv"]
    cst = E["cst"]
    neg1 = cst[0:1, I_NEG1:I_NEG1 + 1]
    st = ExitStack()
    sb = nc.sbuf_tensor
    wv_sb = [st.enter_context(sb(f"wvP5_{i}", [128, CT * 128], BF16)) for i in range(2)]
    wr_sb = [st.enter_context(sb(f"wrP5_{i}", [128, CT * 128], BF16)) for i in range(2)]
    va = [st.enter_context(sb(f"vaP5_{i}", [128, T], F32)) for i in range(2)]
    ra = [st.enter_context(sb(f"raP5_{i}", [128, T], F32)) for i in range(3)]
    B2 = st.enter_context(sb("B2P5", [128, T], F32))
    B2d = st.enter_context(sb("B2dP5", [128, T], F32))
    Bw = st.enter_context(sb("BwP5", [128, T], F32))
    B3d = [st.enter_context(sb(f"B3P5_{i}", [128, T], F32)) for i in range(2)]
    B4d = [st.enter_context(sb(f"B4P5_{i}", [128, T], F32)) for i in range(2)]
    ac = st.enter_context(sb("acP5", [128, 8 * 6], F32))
    negf = cst[:, I_NEG1:I_NEG1 + 1]

    wload = {}

    def load_wvr(gp, dt):
        rot = dt % 2
        g.dma(gp, "gp", _r3(wv_sb[rot][:], CT, 128),
              E["WvT_h"][:, dt * 128:(dt + 1) * 128].rearrange("(a p) n -> p a n", p=128))
        return g.dma(gp, "gp", _r3(wr_sb[rot][:], CT, 128),
                     E["WrT_h"][:, dt * 128:(dt + 1) * 128].rearrange("(a p) n -> p a n", p=128))

    @g.block.gpsimd
    def _(gp):
        wload[0] = load_wvr(gp, 0)
        wload[1] = load_wvr(gp, 1)

    cd = [0]

    @g.block.vector
    def _(ve):
        ve.tensor_scalar_max(ac[:, 0:8], E["td0c"], 0.0)
        ve.tensor_scalar_min(ac[:, 8:16], E["td0c"], 0.0)
        ve.tensor_scalar_max(ac[:, 16:24], E["tf0c"], 0.0)
        cd[0] = g.i(ve.tensor_scalar_min(ac[:, 24:32], E["tf0c"], 0.0), "ve")

    sed = [0]

    @g.block.scalar
    def _(se):
        g.w(se, "ve", cd[0])
        se.activation(ac[:, 0:8], ac[:, 0:8], AF.Exp, scale=negf)
        se.activation(ac[:, 8:16], ac[:, 8:16], AF.Exp)
        se.activation(ac[:, 16:24], ac[:, 16:24], AF.Exp, scale=negf)
        sed[0] = g.i(se.activation(ac[:, 24:32], ac[:, 24:32], AF.Exp), "se")

    @g.block.vector
    def _(ve):
        g.w(ve, "se", sed[0])
        ve.tensor_scalar_add(ac[:, 32:40], ac[:, 24:32], 1e-6)
        g.i(ve.tensor_mul(ac[:, 40:48], ac[:, 16:24], ac[:, 8:16]), "ve")

    bp = dict(g.c)

    SC = {"ve": {}, "se": {}, "pe": {}, "pc": {}}
    cnt = {"ve": 0, "se": 0, "pe": 0, "pc": 0}

    def _ev(eng, key, n=1):
        cnt[eng] += n
        SC[eng][key] = cnt[eng]

    for _dt in range(CT + 1):
        if _dt < CT:
            for _tc in range(NTC):
                _ev("pe", ("mmv", _dt, _tc))
                _ev("pe", ("mmr", _dt, _tc))
            for _tc in range(NTC):
                _ev("se", ("evv", _dt, _tc))
                _ev("se", ("evr", _dt, _tc))
        if _dt >= 1:
            _d = _dt - 1
            _ev("ve", ("scanB", _d))
            _ev("ve", ("aff", _d))
            _ev("ve", ("tssN", _d))
            _ev("ve", ("sttN", _d))
            _ev("se", ("lnexp", _d))
            _ev("pc", ("mul1", _d))
            _ev("pc", ("rwkv", _d))

    def at(eng, *key):
        return bp[eng] + SC[eng][key]

    E["p5_rwkv_at"] = [at("pc", "rwkv", d) for d in range(CT)]

    def emit_chain(d):
        al = ac[:, 0 + d:1 + d]
        be = ac[:, 8 + d:9 + d]
        e1 = ac[:, 16 + d:17 + d]
        e2 = ac[:, 24 + d:25 + d]
        e2e = ac[:, 32 + d:33 + d]
        e1be = ac[:, 40 + d:41 + d]

        @g.block.vector
        def _(ve, d=d):
            if d >= 2:
                g.w(ve, "se", at("se", "lnexp", d - 2))
                g.w(ve, "pc", at("pc", "mul1", d - 2))
            g.w(ve, "se", at("se", "evv", d, NTC - 1))
            B3_ = B3d[d % 2]
            got = g.i(ve.tensor_tensor_scan(B2d[:], al.broadcast_to([128, T]),
                                            be.broadcast_to([128, T]), 0.0,
                                            ALU.mult, ALU.add), "ve")
            assert got == at("ve", "scanB", d)
            B4_ = B4d[d % 2]
            ve.tensor_scalar(B4_[:, 1:T], B2d[:, 0:T - 1], e1, e2e,
                             ALU.mult, ALU.add)
            got = g.i(ve.tensor_copy(B4_[:, 0:1], e2e), "ve")
            assert got == at("ve", "aff", d)
            ve.tensor_tensor_scan(B2[:], al.broadcast_to([128, T]), va[d % 2][:],
                                  0.0, ALU.mult, ALU.add)
            got = g.i(ve.tensor_single_scalar(B3_[:], va[d % 2][:], e2, ALU.mult),
                      "ve")
            assert got == at("ve", "tssN", d)
            got = g.i(ve.scalar_tensor_tensor(B3_[:, 1:T], B2[:, 0:T - 1], e1be,
                                              B3_[:, 1:T], ALU.mult, ALU.add), "ve")
            assert got == at("ve", "sttN", d)

        @g.block.scalar
        def _(se, d=d):
            g.w(se, "ve", at("ve", "aff", d))
            B4_ = B4d[d % 2]
            se.activation(B4_[:], B4_[:], AF.Ln)
            got = g.i(se.activation(B4_[:], B4_[:], AF.Exp, scale=negf), "se")
            assert got == at("se", "lnexp", d)

    def emit_chain_gp(d):
        @g.block.gpsimd
        def _(gp, d=d):
            g.w(gp, "ve", at("ve", "sttN", d))
            g.w(gp, "se", at("se", "lnexp", d))
            B3_, B4_ = B3d[d % 2], B4d[d % 2]
            g.i(gp.tensor_mul(Bw[:], B3_[:], B4_[:]), "pc")
            got = g.i(gp.tensor_mul(rwkv[d][:], ra[d % 3][:], Bw[:]), "pc")
            assert got == at("pc", "rwkv", d)

    for dt in range(CT):
        if dt >= 1:
            emit_chain_gp(dt - 1)
        if dt + 2 < CT:
            @g.block.gpsimd
            def _(gp, dt=dt):
                g.w(gp, "pe", at("pe", "mmr", dt, NTC - 1))
                wload[dt + 2] = load_wvr(gp, dt + 2)

        @g.block.tensor
        def _(pe, dt=dt):
            g.w(pe, "gp", wload[dt])
            for tc in range(NTC):
                pidx = (dt, tc - 2) if tc >= 2 else ((dt - 1, tc + 2) if dt > 0 else None)
                if pidx is not None:
                    g.w(pe, "se", at("se", "evr", *pidx))
                for ct in range(CT):
                    mm = pe.matmul(pb[tc % 2][:],
                                   wv_sb[dt % 2][:, ct * 128:(ct + 1) * 128],
                                   xnT[ct][:, tc * 512:(tc + 1) * 512],
                                   start=(ct == 0), stop=(ct == CT - 1))
                assert g.i(mm, "pe") == at("pe", "mmv", dt, tc)
                for ct in range(CT):
                    mm = pe.matmul(pb[2 + tc % 2][:],
                                   wr_sb[dt % 2][:, ct * 128:(ct + 1) * 128],
                                   xnT[ct][:, tc * 512:(tc + 1) * 512],
                                   start=(ct == 0), stop=(ct == CT - 1))
                assert g.i(mm, "pe") == at("pe", "mmr", dt, tc)

        @g.block.scalar
        def _(se, dt=dt):
            if dt >= 3:
                g.w(se, "pc", at("pc", "rwkv", dt - 3))
            if dt >= 2:
                g.w(se, "ve", at("ve", "tssN", dt - 2))
            for tc in range(NTC):
                g.w(se, "pe", at("pe", "mmv", dt, tc))
                got = g.i(se.copy(va[dt % 2][:, tc * 512:(tc + 1) * 512],
                                  pb[tc % 2][:]), "se")
                assert got == at("se", "evv", dt, tc)
                g.w(se, "pe", at("pe", "mmr", dt, tc))
                got = g.i(se.activation(ra[dt % 3][:, tc * 512:(tc + 1) * 512],
                                        pb[2 + tc % 2][:], AF.Sigmoid), "se")
                assert got == at("se", "evr", dt, tc)

        if dt >= 1:
            emit_chain(dt - 1)

    emit_chain(CT - 1)
    emit_chain_gp(CT - 1)

    st.close()



def _p6_combine_l0(g, E):
    nc = g.nc
    pb = E["pb"]
    rwkv = E["rwkv"]
    cst = E["cst"]
    neg1 = cst[0:1, I_NEG1:I_NEG1 + 1]
    st = ExitStack()
    sb = nc.sbuf_tensor
    WoTs = st.enter_context(sb("WoTsP6", [128, CT * 1024], BF16))
    xt = [st.enter_context(sb(f"xtP6_{i}", [128, C], F32)) for i in range(3)]
    at = [st.enter_context(sb(f"atP6_{i}", [128, C], F32)) for i in range(3)]
    gt = [st.enter_context(sb(f"gtP6_{i}", [128, C], F32)) for i in range(3)]
    x1b = [st.enter_context(sb(f"x1bP6_{i}", [128, C], BF16)) for i in range(3)]
    x1r = [st.enter_context(sb(f"x1rP6_{i}", [128, C], BF16)) for i in range(3)]
    xq = [st.enter_context(sb(f"xqP6_{i}", [128, C], F32)) for i in range(3)]
    tb = st.enter_context(sb("tbP6", [128, 512], F32))
    m1c, r1c = E["m1c"], E["r1c"]
    sc2, sc3 = E["sc2"], E["sc3"]

    wload = [0]

    @g.block.gpsimd
    def _(gp):
        wload[0] = g.dma(gp, "gp", _r3(WoTs[:], CT, 1024),
                         E["WoT_h"].rearrange("(a p) d -> p a d", p=128))

    ld = {}
    smm = {}
    gdone = {}
    x1done = {}
    sqd = {}
    statd = {}
    stored = {}
    storedb = {}
    gsto = {}

    for tt in range(TT):
        rot = tt % 3

        @g.block.gpsimd
        def _(gp, tt=tt, rot=rot):
            if tt >= 3:
                g.w(gp, "ve", sqd[tt - 3])
                g.w(gp, "sq", gsto[tt - 3])
            ld[("x", tt)] = g.dma(gp, "gp", xt[rot][:],
                                  E["x_h"][tt * 128:(tt + 1) * 128, :])

        @g.block.sync
        def _(sy, tt=tt, rot=rot):
            if tt >= 3:
                g.w(sy, "se", gdone[tt - 3])
            ld[("a", tt)] = g.dma(sy, "sy", at[rot][:],
                                  E["acc0_h"][tt * 128:(tt + 1) * 128, :])

        @g.block.tensor
        def _(pe, tt=tt):
            g.w(pe, "gp", wload[0])
            if tt >= 2:
                g.w(pe, "ve", x1done[tt - 2])
            for dc in range(2):
                bank = 2 * (tt % 2) + dc
                for dtc in range(CT):
                    if tt == 0 and dc == 0:
                        g.w(pe, "pc", E["p5_rwkv_at"][dtc])
                    mm = pe.matmul(pb[bank][:],
                                   rwkv[dtc][:, tt * 128:(tt + 1) * 128],
                                   WoTs[:, dtc * 1024 + dc * 512:dtc * 1024 + (dc + 1) * 512],
                                   start=(dtc == 0), stop=(dtc == CT - 1))
            smm[tt] = g.i(mm, "pe")

        @g.block.scalar
        def _(se, tt=tt, rot=rot):
            g.w(se, "sy", ld[("a", tt)])
            if tt >= 3:
                g.w(se, "ve", x1done[tt - 3])
            gdone[tt] = g.i(se.activation(gt[rot][:], at[rot][:], AF.Silu), "se")

        @g.block.vector
        def _(ve, tt=tt, rot=rot):
            g.w(ve, "pe", smm[tt])
            g.w(ve, "se", gdone[tt])
            g.w(ve, "gp", ld[("x", tt)])
            if tt >= 3:
                g.w(ve, "sq", gsto[tt - 3])
            for dc in range(2):
                bank = 2 * (tt % 2) + dc
                ve.tensor_mul(tb[:], pb[bank][:], gt[rot][:, dc * 512:(dc + 1) * 512])
                ve.tensor_add(xt[rot][:, dc * 512:(dc + 1) * 512],
                              xt[rot][:, dc * 512:(dc + 1) * 512], tb[:])
            if tt >= 3:
                g.w(ve, "sy", storedb[tt - 3])
                g.w(ve, "sq", stored[tt - 3])
            ve.tensor_copy(x1b[rot][:], xt[rot][:])
            ve.tensor_sub(x1r[rot][:], xt[rot][:], x1b[rot][:])
            x1done[tt] = g.i(ve.reduce_sum(m1c[:, tt:tt + 1], xt[rot][:], axis=AX), "ve")

        @g.block.vector
        def _(ve, tt=tt, rot=rot):
            ve.tensor_mul(xq[rot][:], xt[rot][:], xt[rot][:])
            sqd[tt] = statd[tt] = g.i(
                ve.reduce_sum(sc2[:, tt:tt + 1], xq[rot][:], axis=AX), "ve")

        @g.block.scalar
        def _(se, tt=tt, rot=rot):
            g.w(se, "ve", x1done[tt])
            gsto[tt] = g.dma(se, "sq", E["x1_h"][tt * 128:(tt + 1) * 128, :],
                             xt[rot][:])
            stored[tt] = g.dma(se, "sq", E["x1r_h"][tt * 128:(tt + 1) * 128, :],
                               x1r[rot][:])

        @g.block.sync
        def _(sy, tt=tt, rot=rot):
            g.w(sy, "ve", x1done[tt])
            storedb[tt] = g.dma(sy, "sy", E["x1b_h"][tt * 128:(tt + 1) * 128, :],
                                x1b[rot][:])

    fin = [0]

    @g.block.vector
    def _(ve):
        g.w(ve, "ve", statd[TT - 1])
        ve.tensor_scalar_mul(m1c[:], m1c[:], 1.0 / C)
        ve.tensor_scalar_mul(sc2[:], sc2[:], 1.0 / C)
        ve.drain()
        ve.tensor_mul(sc3[:], m1c[:], m1c[:])
        ve.drain()
        ve.tensor_sub(sc2[:], sc2[:], sc3[:])
        ve.drain()
        fin[0] = g.i(ve.tensor_copy(E["v1q"][:], sc2[:]), "ve")

    fin2 = [0]

    @g.block.scalar
    def _(se):
        g.w(se, "ve", fin[0])
        se.activation(sc2[:], sc2[:], AF.Sqrt, bias=cst[:, I_EPS:I_EPS + 1])
        se.drain()
        se.activation(sc3[:], sc2[:], AF.Ln)
        se.drain()
        fin2[0] = g.i(se.activation(r1c[:], sc3[:], AF.Exp,
                                    scale=cst[:, I_NEG1:I_NEG1 + 1]), "se")

    _cols_to_hbm_row(g, E, m1c, E["m1r_h"], after_se=fin2[0])
    _cols_to_hbm_row(g, E, r1c, E["r1r_h"], after_se=fin2[0])
    st.close()


def _cols_to_hbm_row(g, E, cols, hbm_row, after_se=0, after_ve=0):
    pb = E["pb"]
    trs = E["trs"]
    done = {}

    @g.block.tensor
    def _(pe):
        g.w(pe, "se", after_se)
        g.w(pe, "ve", after_ve if after_ve else g.c["ve"])
        g.w(pe, "sy", g.c["sy"])
        done["tr"] = g.i(pe.transpose(pb[7][0:16, 0:128], cols[:], E["idn"][:]), "pe")

    @g.block.vector
    def _(ve):
        g.w(ve, "pe", done["tr"])
        done["ev"] = g.i(ve.tensor_copy(trs[:], pb[7][0:16, 0:128]), "ve")

    @g.block.sync
    def _(sy):
        g.w(sy, "ve", done["ev"])
        g.dma(sy, "sy", hbm_row.rearrange("o (a n) -> (o a) n", n=128), trs[:])



def _q12_l1(g, E, l1st, upto):
    nc = g.nc
    pb = E["pb"]
    xnT1 = E["xnT1"]
    mwA = E["mwA"]
    csc = E["csc"]
    sb = nc.sbuf_tensor
    st = l1st
    WeT = [st.enter_context(sb(f"WeTQ2_{i}", [128, CT * 1024], BF16)) for i in range(2)]
    stg = [st.enter_context(sb(f"stgQ2_{i}", [128, 2 * C], F32)) for i in range(2)]
    lgt = st.enter_context(sb("lgtQ2", [128, 2 * NE], F32))
    mwT = st.enter_context(sb("mwTQ2", [8, T], BF16))
    rwf = st.enter_context(sb("rwfQ2", [128, CT * NE], F32))
    m1row = st.enter_context(sb("m1rowQ2", [1, T], F32))
    negS = st.enter_context(sb("negSQ2", [1, NE], F32))
    r2c = st.enter_context(sb("r2cQ12", [128, TT], F32))
    s1c = st.enter_context(sb("s1cQ12", [128, TT], F32))
    s2c = st.enter_context(sb("s2cQ12", [128, TT], F32))
    r1c = E["r1c"]
    m1c = E["m1c"]
    ones = E["cst"][:, I_ONE:I_ONE + 1]

    xq = st.enter_context(sb("xqQ12", [128, T], F32))
    q1x = ExitStack()
    xbt = [q1x.enter_context(sb(f"xbtQ1_{i}", [128, T], BF16)) for i in range(4)]
    xrt = [q1x.enter_context(sb(f"xrtQ1_{i}", [128, T], BF16)) for i in range(4)]
    rwfb = q1x.enter_context(sb("rwfbQ1", [128, CT * NE], BF16))
    rwr = q1x.enter_context(sb("rwrQ1", [128, CT * NE], BF16))
    s1tmp = q1x.enter_context(sb("s1tmpQ1", [128, T], F32))

    m1bc = q1x.enter_context(sb("m1bcQ1", [128, T], BF16))
    r1bc = q1x.enter_context(sb("r1bcQ1", [128, T], BF16))
    std = {}
    sld = [0]

    @g.block.sync
    def _(sy):
        g.dma(sy, "sy", _r3(rwf[:], CT, NE),
              E["rw_h"].rearrange("(a p) e -> p a e", p=128))
        sld[0] = g.dma(sy, "sy", m1row[:], E["m1r_h"])

    wld = {}

    @g.block.gpsimd
    def _(gp):
        wld[0] = g.dma(gp, "gp", _r3(WeT[0][:], CT, 1024),
                       E["ew_h"][0].rearrange("(a p) d -> p a d", p=128))
        wld[1] = g.dma(gp, "gp", _r3(WeT[1][:], CT, 1024),
                       E["ew_h"][1].rearrange("(a p) d -> p a d", p=128))

    lgd = {}
    mwd = {}
    trd = {}
    mwTd = {}
    ld1 = {}
    bcd = [0]

    @g.block.gpsimd
    def _(gp):
        g.dma(gp, "gp", m1bc[:], E["m1r_h"].broadcast_to([128, T]))
        bcd[0] = 0
        g.dma(gp, "gp", r1bc[:], E["r1r_h"].broadcast_to([128, T]))

    rbld = [0]
    sS = {}

    @g.block.gpsimd
    def _(gp):
        rbld[0] = g.dma(gp, "gp", _r3(rwfb[:], CT, NE),
                        E["rw_h"].rearrange("(a p) e -> p a e", p=128))

    @g.block.tensor
    def _(pe):
        g.w(pe, "sy", sld[0])
        for ct in range(CT):
            mm = pe.matmul(pb[3][0:1, 0:NE], ones, rwf[:, ct * NE:(ct + 1) * NE],
                           start=(ct == 0), stop=(ct == CT - 1))
        sS["mm"] = g.i(mm, "pe")

    @g.block.vector
    def _(ve):
        g.w(ve, "pe", sS["mm"])
        g.w(ve, "gp", rbld[0])
        ve.tensor_scalar_mul(negS[:], pb[3][0:1, 0:NE], -1.0)
        ve.drain()
        sS["neg"] = g.i(ve.tensor_sub(rwr[:], rwf[:], rwfb[:]), "ve")

    q1t = {}
    q1v = {}
    subd = {}
    lgmm = {}

    q1tb = {}
    for ct in range(CT):
        @g.block.sync
        def _(sy, ct=ct):
            if ct >= 4:
                g.w(sy, "pc", q1v[ct - 4])
                g.w(sy, "pe", lgmm[ct // 2 - 2])
            q1tb[ct] = g.dma(sy, "sy", xbt[ct % 4][:],
                             E["x1b_h"][:, ct * 128:(ct + 1) * 128], transpose=True)
            q1t[ct] = g.dma(sy, "sy", xrt[ct % 4][:],
                            E["x1r_h"][:, ct * 128:(ct + 1) * 128], transpose=True)

        @g.block.gpsimd
        def _(gp, ct=ct):
            g.w(gp, "sy", q1t[ct])
            subd[ct] = g.i(gp.tensor_sub(s1tmp[:], xbt[ct % 4][:], m1bc[:]), "pc")
            q1v[ct] = g.i(gp.tensor_mul(xnT1[ct][:], s1tmp[:], r1bc[:]), "pc")

        if ct % 2 == 1:
            p = ct // 2

            @g.block.tensor
            def _(pe, ct=ct, p=p):
                g.w(pe, "sy", q1t[ct])
                if p == 0:
                    g.w(pe, "ve", sS["neg"])
                for tt in range(TT):
                    for cc in (ct - 1, ct):
                        last = (cc == ct and p != 3)
                        pe.matmul(pb[4 + p][:, tt * NE:(tt + 1) * NE],
                                  xbt[cc % 4][:, tt * 128:(tt + 1) * 128],
                                  rwfb[:, cc * NE:(cc + 1) * NE],
                                  start=(cc == ct - 1), stop=False)
                        pe.matmul(pb[4 + p][:, tt * NE:(tt + 1) * NE],
                                  xbt[cc % 4][:, tt * 128:(tt + 1) * 128],
                                  rwr[:, cc * NE:(cc + 1) * NE],
                                  start=False, stop=False)
                        mm = pe.matmul(pb[4 + p][:, tt * NE:(tt + 1) * NE],
                                       xrt[cc % 4][:, tt * 128:(tt + 1) * 128],
                                       rwfb[:, cc * NE:(cc + 1) * NE],
                                       start=False, stop=last)
                    if p == 3:
                        mm = pe.matmul(pb[4 + p][:, tt * NE:(tt + 1) * NE],
                                       m1row[0:1, tt * 128:(tt + 1) * 128],
                                       negS[0:1, :], start=False, stop=True)
                lgmm[p] = g.i(mm, "pe")

    bmm = {}
    bev = {}
    acc_hist = []
    std = {}
    for tt in range(TT):
        rot = tt % 2
        co = (tt % 2) * 16

        @g.block.vector
        def _(ve, tt=tt, co=co):
            lg = lgt[:, (tt % 2) * NE:(tt % 2) * NE + NE]
            g.w(ve, "pe", lgmm[3])
            if tt >= 2:
                g.w(ve, "ve", mwd[tt - 2])
            ve.tensor_copy(lg, pb[4][:, tt * NE:(tt + 1) * NE])
            ve.drain()
            ve.tensor_add(lg, lg, pb[5][:, tt * NE:(tt + 1) * NE])
            ve.drain()
            ve.tensor_add(lg, lg, pb[6][:, tt * NE:(tt + 1) * NE])
            ve.drain()
            ve.tensor_add(lg, lg, pb[7][:, tt * NE:(tt + 1) * NE])
            ve.drain()
            mwd[("s", tt)] = g.i(ve.scalar_tensor_tensor(
                lg, lg, r1c[:, tt:tt + 1], E["rbbc"][:], ALU.mult, ALU.add), "ve")
            ve.drain()
            ve.reduce_max(csc[:, co:co + 1], lg, axis=AX)
            ve.drain()
            mwd[("m", tt)] = g.i(ve.tensor_scalar_mul(csc[:, co + 1:co + 2],
                                                      csc[:, co:co + 1], -1.0), "ve")

        @g.block.scalar
        def _(se, tt=tt, co=co):
            lg = lgt[:, (tt % 2) * NE:(tt % 2) * NE + NE]
            g.w(se, "ve", mwd[("m", tt)])
            mwd[("e", tt)] = g.i(
                se.activation(csc[:, co + 2:co + 10], lg, AF.Exp,
                              bias=csc[:, co + 1:co + 2]), "se")

        @g.block.vector
        def _(ve, tt=tt, co=co):
            lg = lgt[:, (tt % 2) * NE:(tt % 2) * NE + NE]
            g.w(ve, "se", mwd[("e", tt)])
            ve.reduce_sum(csc[:, co + 10:co + 11], csc[:, co + 2:co + 10], axis=AX)
            ve.drain()
            ve.reciprocal(csc[:, co + 11:co + 12], csc[:, co + 10:co + 11])
            ve.drain()
            mwd[tt] = g.i(ve.tensor_scalar(mwA[:, tt * NE:(tt + 1) * NE], lg,
                                           csc[:, co:co + 1], csc[:, co + 11:co + 12],
                                           ALU.is_equal, ALU.mult), "ve")

    if upto == "Q1":
        return

    for tt in range(TT):
        @g.block.tensor
        def _(pe, tt=tt):
            g.w(pe, "ve", mwd[tt])
            if tt == 0:
                g.w(pe, "ve", mwd[("s", TT - 1)])
            if tt >= 1:
                g.w(pe, "ve", mwTd[tt - 1])
            trd[tt] = g.i(pe.transpose(pb[4][0:8, 0:128],
                                       mwA[:, tt * NE:(tt + 1) * NE],
                                       E["idn"][:]), "pe")

        @g.block.vector
        def _(ve, tt=tt):
            g.w(ve, "pe", trd[tt])
            mwTd[tt] = g.i(ve.tensor_copy(mwT[:, tt * 128:(tt + 1) * 128],
                                          pb[4][0:8, 0:128]), "ve")

        @g.block.tensor
        def _(pe, tt=tt):
            g.w(pe, "ve", mwTd[tt])
            if tt >= 1:
                g.w(pe, "ve", bev[tt - 1])
            for dc in range(2):
                bmm[(tt, dc)] = g.i(pe.matmul(
                    pb[6 + dc][:],
                    mwT[0:8, tt * 128:(tt + 1) * 128],
                    E["ebb"][0:8, dc * 512:(dc + 1) * 512],
                    start=True, stop=True), "pe")

        @g.block.vector
        def _(ve, tt=tt):
            g.w(ve, "pe", bmm[(tt, 1)])
            slot = (tt // 2) % 2
            if tt % 2 == 0 and len(acc_hist) >= 2:
                q2, c2 = acc_hist[-2]
                g.w(ve, q2, c2)
            for dc in range(2):
                mm = ve.tensor_copy(
                    stg[slot][:, (tt % 2) * C + dc * 512:(tt % 2) * C + (dc + 1) * 512],
                    pb[6 + dc][:])
            bev[tt] = g.i(mm, "ve")

        if tt % 2 == 1:
            q = ("gp", "sy", "sq")[(tt // 2) % 3]
            sec = {"gp": g.block.gpsimd, "sy": g.block.sync,
                   "sq": g.block.scalar}[q]

            @sec
            def _(eng, tt=tt, q=q):
                g.w(eng, "ve", bev[tt])
                slot = (tt // 2) % 2
                acc_hist.append((q, g.dma(
                    eng, q,
                    E["acc1_h"][(tt - 1) * 128:(tt + 1) * 128, :].rearrange(
                        "(s p) d -> p s d", p=128),
                    _r3(stg[slot][:], 2, C))))

    fin = [0]

    @g.block.vector
    def _(ve):
        ve.tensor_mul(s2c[:], E["v1q"][:], r1c[:])
        ve.drain()
        fin[0] = g.i(ve.tensor_mul(s2c[:], s2c[:], r1c[:]), "ve")

    fin2 = [0]

    @g.block.scalar
    def _(se):
        g.w(se, "ve", fin[0])
        se.activation(s2c[:], s2c[:], AF.Sqrt, bias=E["cst"][:, I_EPS:I_EPS + 1])
        se.drain()
        se.activation(s1c[:], s2c[:], AF.Ln)
        se.drain()
        fin2[0] = g.i(se.activation(r2c[:], s1c[:], AF.Exp,
                                    scale=E["cst"][:, I_NEG1:I_NEG1 + 1]), "se")

    fin3 = [0]

    @g.block.vector
    def _(ve):
        g.w(ve, "se", fin2[0])
        ve.tensor_mul(s1c[:], r1c[:], r2c[:])
        ve.drain()
        ve.tensor_mul(s2c[:], m1c[:], r1c[:])
        ve.drain()
        fin3[0] = g.i(ve.tensor_mul(s2c[:], s2c[:], r2c[:]), "ve")

    _cols_to_hbm_row(g, E, s1c, E["p4r_h"], after_se=fin2[0], after_ve=fin3[0])
    _cols_to_hbm_row(g, E, s2c, E["q4r_h"], after_se=fin2[0], after_ve=fin3[0])

    q1x.close()
    xbt2 = [st.enter_context(sb(f"xbt2Q12_{i}", [128, T], BF16)) for i in range(2)]
    Pbc = st.enter_context(sb("PbcQ12", [128, T], F32))
    Qbc = st.enter_context(sb("QbcQ12", [128, T], F32))

    bcd2 = [0]

    @g.block.sync
    def _(sy):
        g.w(sy, "pc", q1v[CT - 1])
        g.w(sy, "pe", lgmm[3])
        g.dma(sy, "sy", Pbc[:], E["p4r_h"].broadcast_to([128, T]))
        bcd2[0] = g.dma(sy, "sy", Qbc[:], E["q4r_h"].broadcast_to([128, T]))

    ymm = {}
    yev = {}
    ve_gate_done = g.c["ve"]
    for e in range(NE):
        for tt in range(TT):
            gbase = e * TT + tt

            @g.block.tensor
            def _(pe, e=e, tt=tt, gbase=gbase):
                if tt == 0:
                    g.w(pe, "gp", wld[e])
                    if e == 0:
                        g.w(pe, "pc", q1v[CT - 1])
                        g.w(pe, "ve", sS["neg"])
                for dc in range(2):
                    gi = gbase * 2 + dc
                    if gi >= 4:
                        g.w(pe, "ve", yev[(gi - 4) // 2])
                    for ct in range(CT):
                        mm = pe.matmul(
                            pb[gi % 4][:],
                            xnT1[ct][:, tt * 128:(tt + 1) * 128],
                            WeT[e % 2][:, ct * 1024 + dc * 512:ct * 1024 + (dc + 1) * 512],
                            start=(ct == 0), stop=(ct == CT - 1))
                    ymm[gi] = g.i(mm, "pe")

            @g.block.vector
            def _(ve, e=e, tt=tt, gbase=gbase):
                g.w(ve, "pe", ymm[gbase * 2 + 1])
                pair = gbase // 2
                slot = pair % 2
                if tt % 2 == 0:
                    if pair >= 2:
                        q2, c2 = acc_hist[8 + pair - 2]
                    else:
                        q2, c2 = acc_hist[6 + pair]
                    g.w(ve, q2, c2)
                wcol = mwA[:, tt * NE + e:tt * NE + e + 1]
                for dc in range(2):
                    gi = gbase * 2 + dc
                    mm = ve.tensor_single_scalar(
                        stg[slot][:, (tt % 2) * C + dc * 512:(tt % 2) * C + (dc + 1) * 512],
                        pb[gi % 4][:], wcol, ALU.mult)
                yev[gbase] = g.i(mm, "ve")

            if tt % 2 == 1:
                @g.block.gpsimd
                def _(gp, e=e, tt=tt, gbase=gbase):
                    g.w(gp, "ve", yev[gbase])
                    if e == 0:
                        q2, c2 = acc_hist[tt // 2]
                        g.w(gp, q2, c2)
                    slot = (gbase // 2) % 2
                    acc_hist.append(("gp", g.dma(
                        gp, "gp",
                        E["acc1_h"][(tt - 1) * 128:(tt + 1) * 128, :].rearrange(
                            "(s p) d -> p s d", p=128),
                        _r3(stg[slot][:], 2, C), accum=ALU.add)))

        if e + 2 < NE:
            @g.block.gpsimd
            def _(gp, e=e):
                g.w(gp, "pe", ymm[(e * TT + TT - 1) * 2 + 1])
                wld[e + 2] = g.dma(gp, "gp", _r3(WeT[e % 2][:], CT, 1024),
                                   E["ew_h"][e + 2].rearrange("(a p) d -> p a d", p=128))

        if e == 0:
            b1d = {}
            b1v = {}
            for ct in range(CT):
                @g.block.sync
                def _(sy, ct=ct):
                    if ct >= 2:
                        g.w(sy, "ve", b1v[ct - 2])
                    b1d[ct] = g.dma(sy, "sy", xbt2[ct % 2][:],
                                    E["x1b_h"][:, ct * 128:(ct + 1) * 128],
                                    transpose=True)

                @g.block.vector
                def _(ve, ct=ct):
                    g.w(ve, "sy", max(b1d[ct], bcd2[0]))
                    ve.tensor_mul(xq[:], xbt2[ct % 2][:], Pbc[:])
                    ve.tensor_sub(xq[:], xq[:], Qbc[:])
                    b1v[ct] = g.i(ve.tensor_scalar(E["xn1T"][ct][:], xq[:],
                                                   E["g1c"][:, ct:ct + 1],
                                                   E["b1c"][:, ct:ct + 1],
                                                   ALU.mult, ALU.add), "ve")



def _q3_timemix_l1(g, E):
    nc = g.nc
    pb = E["pb"]
    cst = E["cst"]
    neg1 = cst[0:1, I_NEG1:I_NEG1 + 1]
    rwkv1 = E["rwkv1"]
    st = ExitStack()
    sb = nc.sbuf_tensor

    st3 = ExitStack()
    wp = [[st3.enter_context(sb(f"wpQ3_{w}_{i}", [128, CT * 128], BF16))
           for i in range(2)] for w in range(4)]
    ra3 = [st3.enter_context(sb(f"ra3Q3_{i}", [128, T], F32)) for i in range(3)]
    va3 = [st3.enter_context(sb(f"va3Q3_{i}", [128, T], F32)) for i in range(2)]
    ks3 = [st3.enter_context(sb(f"ks3Q3_{i}", [128, T], F32)) for i in range(2)]
    ewb = st3.enter_context(sb("ewbQ3", [128, T], F32))
    S1 = ks3
    S2 = st3.enter_context(sb("S2Q3", [128, T], F32))
    S2p = st3.enter_context(sb("S2pQ3", [128, T], F32))
    S3 = st3.enter_context(sb("S3Q3", [128, T], F32))
    S2d = st3.enter_context(sb("S2dQ3", [128, T], F32))
    dce = st3.enter_context(sb("dceQ3", [128, CT], F32))

    @g.block.scalar
    def _(se):
        g.i(se.activation(dce[:], E["td1c"], AF.Exp), "se")

    wsrc = {0: "Wr1_h", 1: "Ww1_h", 2: "Wk1_h", 3: "Wv1_h"}
    wld = {}

    def load_wp(gp, dt):
        rot = dt % 2
        for w in range(4):
            last = g.dma(gp, "gp", _r3(wp[w][rot][:], CT, 128),
                         E[wsrc[w]][:, dt * 128:(dt + 1) * 128].rearrange(
                             "(a p) n -> p a n", p=128))
        return last

    @g.block.gpsimd
    def _(gp):
        wld[0] = load_wp(gp, 0)
        wld[1] = load_wp(gp, 1)

    bp = dict(g.c)

    SC = {"ve": {}, "se": {}, "pe": {}, "pc": {}}
    cnt = {"ve": 0, "se": 0, "pe": 0, "pc": 0}

    def _ev(eng, key, n=1):
        cnt[eng] += n
        SC[eng][key] = cnt[eng]

    for _dt in range(CT + 1):
        _d = _dt - 1
        if _dt < CT:
            for _w in range(4):
                for _tc in range(NTC):
                    _ev("pe", ("mmp", _dt, _w, _tc))
            for _tc in range(NTC):
                _ev("se", ("evr", _dt, _tc))
            if _dt >= 1:
                _ev("se", ("e", _d))
            for _tc in range(NTC):
                _ev("se", ("evw", _dt, _tc))
            if _dt >= 1:
                _ev("se", ("lnexp", _d))
            for _tc in range(NTC):
                _ev("se", ("vcp", _dt, _tc))
            if _dt >= 1:
                _ev("ve", ("pm0", _d))
                _ev("ve", ("scanA", _d))
                _ev("ve", ("scanB", _d))
            for _tc in range(NTC):
                _ev("ve", ("sub", _dt, _tc))
        else:
            _ev("se", ("e", _d))
            _ev("se", ("lnexp", _d))
            _ev("ve", ("pm0", _d))
            _ev("ve", ("scanA", _d))
            _ev("ve", ("scanB", _d))
        if _dt >= 1:
            _ev("pc", ("mul1", _d))
            _ev("pc", ("rwkv", _d))

    def at(eng, *key):
        return bp[eng] + SC[eng][key]

    E["q3_rwkv_at"] = [at("pc", "rwkv", d) for d in range(CT)]
    bank_consumer = {0: ("se", "evr"), 1: ("se", "evw"),
                     2: ("ve", "sub"), 3: ("se", "vcp")}

    def emit_e(se, d):
        g.w(se, "ve", at("ve", "sub", d, NTC - 1))
        if d >= 2:
            g.w(se, "ve", at("ve", "scanB", d - 2))
        assert g.i(se.activation(S1[d % 2][:], ks3[d % 2][:], AF.Exp), "se") \
            == at("se", "e", d)

    def emit_lnexp(se, d):
        g.w(se, "ve", at("ve", "scanB", d))
        se.activation(S2d[:], S2d[:], AF.Ln, bias=cst[:, I_E8:I_E8 + 1])
        got = g.i(se.activation(S2d[:], S2d[:], AF.Exp,
                                scale=cst[:, I_NEG1:I_NEG1 + 1]), "se")
        assert got == at("se", "lnexp", d)

    def emit_scans(ve, d):
        g.w(ve, "se", at("se", "e", d))
        got = g.i(ve.tensor_mul(S2p[:], S1[d % 2][:], va3[d % 2][:]), "ve")
        assert got == at("ve", "pm0", d)
        if d >= 1:
            g.w(ve, "se", at("se", "lnexp", d - 1))
            g.w(ve, "pc", at("pc", "mul1", d - 1))
        dcol = dce[:, d:d + 1]
        got = g.i(ve.tensor_tensor_scan(S3[:], dcol.broadcast_to([128, T]),
                                        S2p[:], 0.0, ALU.mult, ALU.add), "ve")
        assert got == at("ve", "scanA", d)
        got = g.i(ve.tensor_tensor_scan(S2d[:], dcol.broadcast_to([128, T]),
                                        S1[d % 2][:], 0.0, ALU.mult, ALU.add), "ve")
        assert got == at("ve", "scanB", d)

    def emit_chain_gp(d):
        @g.block.gpsimd
        def _(gp, d=d):
            g.w(gp, "se", at("se", "lnexp", d))
            g.i(gp.tensor_mul(S2[:], S3[:], S2d[:]), "pc")
            got = g.i(gp.tensor_mul(rwkv1[d][:], ra3[d % 3][:], S2[:]), "pc")
            assert got == at("pc", "rwkv", d)

    for dt in range(CT):
        if dt >= 1:
            emit_chain_gp(dt - 1)
        if dt + 2 < CT:
            @g.block.gpsimd
            def _(gp, dt=dt):
                g.w(gp, "pe", at("pe", "mmp", dt, 3, NTC - 1))
                wld[dt + 2] = load_wp(gp, dt + 2)
                if dt == 0:
                    E["wo1_ld"] = g.dma(gp, "gp", _r3(E["Wo1s"][:], CT, 1024),
                                        E["Wo1_h"].rearrange("(a p) d -> p a d",
                                                             p=128))

        @g.block.tensor
        def _(pe, dt=dt):
            g.w(pe, "gp", wld[dt])
            for w in range(4):
                for tc in range(NTC):
                    bank = 2 * w + tc % 2
                    pidx = (dt, tc - 2) if tc >= 2 else ((dt - 1, tc + 2) if dt > 0 else None)
                    if pidx is not None:
                        sname, ev = bank_consumer[w]
                        g.w(pe, sname, at(sname, ev, *pidx))
                    for ct in range(CT):
                        mm = pe.matmul(pb[bank][:],
                                       wp[w][dt % 2][:, ct * 128:(ct + 1) * 128],
                                       E["xn1T"][ct][:, tc * 512:(tc + 1) * 512],
                                       start=(ct == 0), stop=(ct == CT - 1))
                    assert g.i(mm, "pe") == at("pe", "mmp", dt, w, tc)

        @g.block.scalar
        def _(se, dt=dt):
            if dt >= 3:
                g.w(se, "pc", at("pc", "rwkv", dt - 3))
                pass
            if dt >= 2:
                g.w(se, "ve", at("ve", "pm0", dt - 2))
            for tc in range(NTC):
                g.w(se, "pe", at("pe", "mmp", dt, 0, tc))
                got = g.i(se.activation(ra3[dt % 3][:, tc * 512:(tc + 1) * 512],
                                        pb[tc % 2][:], AF.Sigmoid), "se")
                assert got == at("se", "evr", dt, tc)
            if dt >= 1:
                emit_e(se, dt - 1)
                g.w(se, "ve", at("ve", "sub", dt - 1, NTC - 1))
            for tc in range(NTC):
                g.w(se, "pe", at("pe", "mmp", dt, 1, tc))
                got = g.i(se.activation(ewb[:, tc * 512:(tc + 1) * 512],
                                        pb[2 + tc % 2][:], AF.Exp), "se")
                assert got == at("se", "evw", dt, tc)
            if dt >= 1:
                emit_lnexp(se, dt - 1)
            for tc in range(NTC):
                g.w(se, "pe", at("pe", "mmp", dt, 3, tc))
                got = g.i(se.copy(va3[dt % 2][:, tc * 512:(tc + 1) * 512],
                                  pb[6 + tc % 2][:]), "se")
                assert got == at("se", "vcp", dt, tc)

        @g.block.vector
        def _(ve, dt=dt):
            if dt >= 1:
                emit_scans(ve, dt - 1)
            if dt >= 2:
                g.w(ve, "se", at("se", "e", dt - 2))
            for tc in range(NTC):
                g.w(ve, "pe", at("pe", "mmp", dt, 2, tc))
                g.w(ve, "se", at("se", "evw", dt, tc))
                got = g.i(ve.tensor_sub(ks3[dt % 2][:, tc * 512:(tc + 1) * 512],
                                        pb[4 + tc % 2][:],
                                        ewb[:, tc * 512:(tc + 1) * 512]), "ve")
                assert got == at("ve", "sub", dt, tc)

    @g.block.scalar
    def _(se):
        emit_e(se, CT - 1)
        emit_lnexp(se, CT - 1)

    @g.block.vector
    def _(ve):
        emit_scans(ve, CT - 1)

    emit_chain_gp(CT - 1)

    st3.close()
    st.close()



def _q4a_xx(g, E):
    nc = g.nc
    pb = E["pb"]
    cst = E["cst"]
    neg1 = cst[0:1, I_NEG1:I_NEG1 + 1]
    rwkv1 = E["rwkv1"]
    m1c, r1c = E["m1c"], E["r1c"]
    sc2, sc3 = E["sc2"], E["sc3"]
    st = ExitStack()
    sb = nc.sbuf_tensor
    Wo1s = E["Wo1s"]
    x1t = [st.enter_context(sb(f"x1tQ4_{i}", [128, C], F32)) for i in range(3)]
    xxt = [st.enter_context(sb(f"xxtQ4_{i}", [128, C], F32)) for i in range(3)]
    xxb = [st.enter_context(sb(f"xxbQ4_{i}", [128, C], BF16)) for i in range(3)]
    xq = [st.enter_context(sb(f"xqQ4_{i}", [128, C], F32)) for i in range(3)]
    m3c = st.enter_context(sb("m3cQ4", [128, TT], F32))
    r3c = st.enter_context(sb("r3cQ4", [128, TT], F32))

    ld = {}
    tmm = {}
    xxd = {}
    sqd = {}
    statd = {}
    stored = {}
    gsto = {}

    for tt in range(TT):
        rot = tt % 3

        @g.block.gpsimd
        def _(gp, tt=tt, rot=rot):
            if tt >= 3:
                g.w(gp, "ve", xxd[tt - 3])
            ld[tt] = g.dma(gp, "gp", x1t[rot][:], E["x1_h"][tt * 128:(tt + 1) * 128, :])

        @g.block.tensor
        def _(pe, tt=tt):
            g.w(pe, "gp", E["wo1_ld"])
            if tt >= 2:
                g.w(pe, "ve", xxd[tt - 2])
            for dc in range(2):
                bank = 2 * (tt % 2) + dc
                for dtc in range(CT):
                    mm = pe.matmul(pb[bank][:],
                                   rwkv1[dtc][:, tt * 128:(tt + 1) * 128],
                                   Wo1s[:, dtc * 1024 + dc * 512:dtc * 1024 + (dc + 1) * 512],
                                   start=(dtc == 0), stop=(dtc == CT - 1))
            tmm[tt] = g.i(mm, "pe")

        @g.block.vector
        def _(ve, tt=tt, rot=rot):
            g.w(ve, "gp", ld[tt])
            g.w(ve, "pe", tmm[tt])
            if tt >= 3:
                g.w(ve, "ve", sqd[tt - 3])
                g.w(ve, "sq", gsto[tt - 3])
                g.w(ve, "sy", stored[tt - 3])
            ve.tensor_scalar(xxt[rot][:], x1t[rot][:], m1c[:, tt:tt + 1],
                             r1c[:, tt:tt + 1], ALU.subtract, ALU.mult)
            for dc in range(2):
                bank = 2 * (tt % 2) + dc
                ve.tensor_add(xxt[rot][:, dc * 512:(dc + 1) * 512],
                              xxt[rot][:, dc * 512:(dc + 1) * 512], pb[bank][:])
            ve.tensor_copy(xxb[rot][:], xxt[rot][:])
            xxd[tt] = g.i(ve.reduce_sum(m3c[:, tt:tt + 1], xxt[rot][:], axis=AX), "ve")

        @g.block.vector
        def _(ve, tt=tt, rot=rot):
            ve.tensor_mul(xq[rot][:], xxt[rot][:], xxt[rot][:])
            sqd[tt] = statd[tt] = g.i(
                ve.reduce_sum(sc2[:, tt:tt + 1], xq[rot][:], axis=AX), "ve")

        @g.block.scalar
        def _(se, tt=tt, rot=rot):
            g.w(se, "ve", xxd[tt])
            gsto[tt] = g.dma(se, "sq", E["xx_h"][tt * 128:(tt + 1) * 128, :],
                             xxt[rot][:])

        @g.block.sync
        def _(sy, tt=tt, rot=rot):
            g.w(sy, "ve", xxd[tt])
            stored[tt] = g.dma(sy, "sy", E["xxb_h"][tt * 128:(tt + 1) * 128, :],
                               xxb[rot][:])

    fin = [0]

    @g.block.vector
    def _(ve):
        g.w(ve, "ve", statd[TT - 1])
        ve.tensor_scalar_mul(m3c[:], m3c[:], 1.0 / C)
        ve.tensor_scalar_mul(sc2[:], sc2[:], 1.0 / C)
        ve.drain()
        ve.tensor_mul(sc3[:], m3c[:], m3c[:])
        ve.drain()
        fin[0] = g.i(ve.tensor_sub(sc2[:], sc2[:], sc3[:]), "ve")

    fin2 = [0]

    @g.block.scalar
    def _(se):
        g.w(se, "ve", fin[0])
        se.activation(sc2[:], sc2[:], AF.Sqrt, bias=cst[:, I_EPS:I_EPS + 1])
        se.drain()
        se.activation(sc3[:], sc2[:], AF.Ln)
        se.drain()
        fin2[0] = g.i(se.activation(r3c[:], sc3[:], AF.Exp,
                                    scale=cst[:, I_NEG1:I_NEG1 + 1]), "se")

    _cols_to_hbm_row(g, E, m3c, E["m3r_h"], after_se=fin2[0])
    _cols_to_hbm_row(g, E, r3c, E["r3r_h"], after_se=fin2[0])
    st.close()



def _q4b_final(g, E):
    nc = g.nc
    pb = E["pb"]
    st = ExitStack()
    sb = nc.sbuf_tensor

    xxnT = [st.enter_context(sb(f"xxnT_{i}", [128, T], BF16)) for i in range(CT)]
    st2 = ExitStack()
    xbt = [st2.enter_context(sb(f"xbtQ4_{i}", [128, T], BF16)) for i in range(2)]
    s1 = st2.enter_context(sb("s1Q4", [128, T], F32))
    m3bc = st2.enter_context(sb("m3bcQ4", [128, T], F32))
    r3bc = st2.enter_context(sb("r3bcQ4", [128, T], F32))

    bcd = [0]

    @g.block.sync
    def _(sy):
        g.dma(sy, "sy", m3bc[:], E["m3r_h"].broadcast_to([128, T]))
        bcd[0] = g.dma(sy, "sy", r3bc[:], E["r3r_h"].broadcast_to([128, T]))

    trd = {}
    vd = {}
    for ct in range(CT):
        @g.block.sync
        def _(sy, ct=ct):
            if ct >= 2:
                g.w(sy, "ve", vd[ct - 2])
            trd[ct] = g.dma(sy, "sy", xbt[ct % 2][:],
                            E["xxb_h"][:, ct * 128:(ct + 1) * 128], transpose=True)

        @g.block.vector
        def _(ve, ct=ct):
            g.w(ve, "sy", max(trd[ct], bcd[0]))
            ve.tensor_sub(s1[:], xbt[ct % 2][:], m3bc[:])
            ve.tensor_mul(s1[:], s1[:], r3bc[:])
            vd[ct] = g.i(ve.tensor_scalar(xxnT[ct][:], s1[:], E["g2c"][:, ct:ct + 1],
                                          E["b2c"][:, ct:ct + 1], ALU.mult, ALU.add), "ve")

    g.barrier()
    st2.close()

    st3 = ExitStack()
    WrcS = st3.enter_context(sb("WrcSQ4", [128, CT * 1024], BF16))
    wky = [st3.enter_context(sb(f"wkyQ4_{i}", [128, CT * 512], BF16)) for i in range(2)]
    wvl = [st3.enter_context(sb(f"wvlQ4_{i}", [128, 4 * 1024], BF16)) for i in range(2)]
    kcm = [st3.enter_context(sb(f"kcmQ4_{i}", [128, 512], BF16)) for i in range(32)]
    rl = st3.enter_context(sb("rlQ4", [128, 2 * 512], F32))
    rcm = st3.enter_context(sb("rcmQ4", [128, 4 * C], F32))
    xxt2 = [st3.enter_context(sb(f"xxt2Q4_{i}", [128, C], F32)) for i in range(2)]
    a1t = [st3.enter_context(sb(f"a1tQ4_{i}", [128, C], F32)) for i in range(2)]
    g1t = [st3.enter_context(sb(f"g1tQ4_{i}", [128, C], F32)) for i in range(2)]
    x1t = [st3.enter_context(sb(f"x1cQ4_{i}", [128, C], F32)) for i in range(2)]
    tb = st3.enter_context(sb("tbQ4", [128, 512], F32))

    wrcl = [0]

    @g.block.gpsimd
    def _(gp):
        wrcl[0] = g.dma(gp, "gp", _r3(WrcS[:], CT, 1024),
                        E["Wrc_h"].rearrange("(a p) d -> p a d", p=128))

    def load_wky(gp, fg, rot):
        return g.dma(gp, "gp", _r3(wky[rot][:], CT, 512),
                     E["Wky_h"][:, fg * 512:(fg + 1) * 512].rearrange(
                         "(a p) f -> p a f", p=128))

    def load_wvl(gp, fg, rot):
        return g.dma(gp, "gp", _r3(wvl[rot][:], 4, 1024),
                     E["Wvl_h"][fg * 512:(fg + 1) * 512, :].rearrange(
                         "(a p) d -> p a d", p=128))

    kyd = {}
    vld = {}
    rvd = {}
    rmm = {}
    red = {}
    kmm = {}
    krl = {}
    kbf = {}
    vmm = {}
    ldd = {}
    sild = {}
    cmbd = {}
    sto = {}

    for tc in range(NTC):
        for gi in range(8):
            sub, dc = gi // 2, gi % 2

            @g.block.tensor
            def _(pe, tc=tc, gi=gi, sub=sub, dc=dc):
                if gi == 0:
                    g.w(pe, "gp", wrcl[0])
                    if tc > 0:
                        g.w(pe, "ve", rvd[tc - 1])
                pv = (tc, gi - 2) if gi >= 2 else ((tc - 1, gi + 6) if tc > 0 else None)
                if pv is not None:
                    g.w(pe, "se", red[pv])
                for ct in range(CT):
                    mm = pe.matmul(pb[2 + gi % 2][:],
                                   xxnT[ct][:, tc * 512 + sub * 128:tc * 512 + (sub + 1) * 128],
                                   WrcS[:, ct * 1024 + dc * 512:ct * 1024 + (dc + 1) * 512],
                                   start=(ct == 0), stop=(ct == CT - 1))
                rmm[(tc, gi)] = g.i(mm, "pe")

            @g.block.scalar
            def _(se, tc=tc, gi=gi, sub=sub, dc=dc):
                g.w(se, "pe", rmm[(tc, gi)])
                if tc > 0:
                    g.w(se, "ve", cmbd[(tc - 1, sub)])
                red[(tc, gi)] = g.i(
                    se.activation(rcm[:, sub * 1024 + dc * 512:sub * 1024 + (dc + 1) * 512],
                                  pb[2 + gi % 2][:], AF.Sigmoid), "se")

        if tc == 0:
            @g.block.gpsimd
            def _(gp):
                kyd[(0, 0)] = load_wky(gp, 0, 0)
                kyd[(0, 1)] = load_wky(gp, 1, 1)

        for ft in range(32):
            fg, fi = ft // 4, ft % 4

            @g.block.tensor
            def _(pe, tc=tc, ft=ft, fg=fg, fi=fi):
                if fi == 0:
                    g.w(pe, "gp", kyd[(tc, fg)])
                pv = (tc, ft - 2) if ft >= 2 else ((tc - 1, ft + 30) if tc > 0 else None)
                if pv is not None:
                    g.w(pe, "se", krl[pv])
                if ft == 0 and tc > 0:
                    g.w(pe, "ve", rvd[tc - 1])
                for ct in range(CT):
                    mm = pe.matmul(pb[ft % 2][:],
                                   wky[fg % 2][:, ct * 512 + fi * 128:ct * 512 + (fi + 1) * 128],
                                   xxnT[ct][:, tc * 512:(tc + 1) * 512],
                                   start=(ct == 0), stop=(ct == CT - 1))
                kmm[(tc, ft)] = g.i(mm, "pe")

            if fi == 3:
                if fg + 2 < 8:
                    @g.block.gpsimd
                    def _(gp, tc=tc, fg=fg):
                        g.w(gp, "pe", kmm[(tc, fg * 4 + 3)])
                        kyd[(tc, fg + 2)] = load_wky(gp, fg + 2, fg % 2)
                elif tc + 1 < NTC:
                    @g.block.gpsimd
                    def _(gp, tc=tc, fg=fg):
                        g.w(gp, "pe", kmm[(tc, fg * 4 + 3)])
                        kyd[(tc + 1, fg % 2)] = load_wky(gp, fg % 2, fg % 2)

            @g.block.scalar
            def _(se, tc=tc, ft=ft):
                g.w(se, "pe", kmm[(tc, ft)])
                pv = (tc, ft - 2) if ft >= 2 else ((tc - 1, ft + 30) if tc > 0 else None)
                if pv is not None:
                    g.w(se, "ve", kbf[pv])
                krl[(tc, ft)] = g.i(
                    se.activation(rl[:, (ft % 2) * 512:(ft % 2 + 1) * 512],
                                  pb[ft % 2][:], AF.Relu), "se")

            @g.block.vector
            def _(ve, tc=tc, ft=ft):
                g.w(ve, "se", krl[(tc, ft)])
                if ft == 0 and tc > 0:
                    g.w(ve, "pe", vmm[(tc - 1, 7)])
                kbf[(tc, ft)] = g.i(
                    ve.tensor_mul(kcm[ft][:],
                                  rl[:, (ft % 2) * 512:(ft % 2 + 1) * 512],
                                  rl[:, (ft % 2) * 512:(ft % 2 + 1) * 512]), "ve")

        if tc == 0:
            @g.block.gpsimd
            def _(gp):
                vld[(0, 0)] = load_wvl(gp, 0, 0)
                vld[(0, 1)] = load_wvl(gp, 1, 1)

        for fg in range(8):
            @g.block.tensor
            def _(pe, tc=tc, fg=fg):
                g.w(pe, "gp", vld[(tc, fg)])
                if fg == 0:
                    g.w(pe, "se", krl[(tc, 31)])
                    g.w(pe, "se", red[(tc, 7)])
                    g.w(pe, "ve", kbf[(tc, 31)])
                for fi in range(4):
                    ft = fg * 4 + fi
                    for j in range(8):
                        sub, dc = j // 2, j % 2
                        mm = pe.matmul(pb[j][:],
                                       kcm[ft][:, sub * 128:(sub + 1) * 128],
                                       wvl[fg % 2][:, fi * 1024 + dc * 512:fi * 1024 + (dc + 1) * 512],
                                       start=(ft == 0), stop=(ft == 31))
                vmm[(tc, fg)] = g.i(mm, "pe")

            if fg + 2 < 8:
                @g.block.gpsimd
                def _(gp, tc=tc, fg=fg):
                    g.w(gp, "pe", vmm[(tc, fg)])
                    vld[(tc, fg + 2)] = load_wvl(gp, fg + 2, fg % 2)
            elif tc + 1 < NTC:
                @g.block.gpsimd
                def _(gp, tc=tc, fg=fg):
                    g.w(gp, "pe", vmm[(tc, fg)])
                    vld[(tc + 1, fg % 2)] = load_wvl(gp, fg % 2, fg % 2)

        @g.block.vector
        def _(ve, tc=tc):
            g.w(ve, "pe", vmm[(tc, 7)])
            for sub in range(4):
                for dc in range(2):
                    j = sub * 2 + dc
                    sl = rcm[:, sub * 1024 + dc * 512:sub * 1024 + (dc + 1) * 512]
                    mm = ve.tensor_mul(sl, pb[j][:], sl)
            rvd[tc] = g.i(mm, "ve")

        for sub in range(4):
            tt = tc * 4 + sub
            rot = sub % 2

            @g.block.sync
            def _(sy, tc=tc, sub=sub, tt=tt, rot=rot):
                if tt == 0:
                    ldd[(0, "xx")] = g.dma(sy, "sy", xxt2[0][:],
                                           E["xx_h"][0:128, :])
                    ldd[(0, "x1")] = g.dma(sy, "sy", x1t[0][:],
                                           E["x1_h"][0:128, :])
                if tt + 1 < NTC * 4:
                    if tt >= 1:
                        g.w(sy, "ve", cmbd[((tt - 1) // 4, (tt - 1) % 4)])
                    nrot = (tt + 1) % 2
                    ldd[(tt + 1, "xx")] = g.dma(
                        sy, "sy", xxt2[nrot][:],
                        E["xx_h"][(tt + 1) * 128:(tt + 2) * 128, :])
                    ldd[(tt + 1, "x1")] = g.dma(
                        sy, "sy", x1t[nrot][:],
                        E["x1_h"][(tt + 1) * 128:(tt + 2) * 128, :])

            @g.block.scalar
            def _(se, tc=tc, sub=sub, tt=tt, rot=rot):
                if tt == 0:
                    ldd[(0, "a")] = g.dma(se, "sq", a1t[0][:],
                                          E["acc1_h"][0:128, :])
                g.w(se, "sq", ldd[(tt, "a")])
                sild[(tc, sub)] = g.i(se.activation(g1t[rot][:], a1t[rot][:], AF.Silu), "se")
                if tt >= 1:
                    pp = ((tt - 1) // 4, (tt - 1) % 4)
                    g.w(se, "ve", cmbd[pp])
                    sto[pp] = g.dma(se, "sq",
                                    E["out_h"][(tt - 1) * 128:tt * 128, :],
                                    a1t[(tt - 1) % 2][:])
                if tt + 1 < NTC * 4:
                    ldd[(tt + 1, "a")] = g.dma(
                        se, "sq", a1t[(tt + 1) % 2][:],
                        E["acc1_h"][(tt + 1) * 128:(tt + 2) * 128, :])

            @g.block.vector
            def _(ve, tc=tc, sub=sub, tt=tt, rot=rot):
                g.w(ve, "se", sild[(tc, sub)])
                g.w(ve, "sy", ldd[(tt, "x1")])
                ve.tensor_add(xxt2[rot][:], xxt2[rot][:],
                              rcm[:, sub * 1024:(sub + 1) * 1024])
                ve.tensor_mul(a1t[rot][:], xxt2[rot][:], g1t[rot][:])
                cmbd[(tc, sub)] = g.i(ve.tensor_add(a1t[rot][:], a1t[rot][:], x1t[rot][:]), "ve")

    @g.block.scalar
    def _(se):
        lt = NTC * 4 - 1
        g.w(se, "ve", cmbd[(NTC - 1, 3)])
        sto[(NTC - 1, 3)] = g.dma(se, "sq", E["out_h"][lt * 128:(lt + 1) * 128, :],
                                  a1t[lt % 2][:])

    st3.close()
    st.close()



def _prep_shared(inputs):
    f32 = np.float32

    def t(a):
        return np.ascontiguousarray(np.asarray(a, dtype=f32))

    def tr(a):
        return np.ascontiguousarray(np.asarray(a, dtype=f32).T)

    def col8(a):
        return np.ascontiguousarray(np.asarray(a, dtype=f32).reshape(CT, 128).T)
    col8_ref = [col8]

    cst = np.zeros((128, 8), f32)
    cst[:, I_ONE] = 1.0
    cst[:, I_NEG1] = -1.0
    cst[:, I_EPS] = LN_EPS
    cst[:, I_E6] = 1e-6
    cst[:, I_E8] = 1e-8
    smal = np.concatenate([
        cst,
        col8_ref[0](inputs["L0_attn_time_decay"]),
        col8_ref[0](inputs["L0_attn_time_first"]),
        col8_ref[0](inputs["L1_attn_time_decay"]),
        col8_ref[0](inputs["L1_attn_ln1_g"]),
        col8_ref[0](inputs["L1_attn_ln1_b"]),
        col8_ref[0](inputs["L1_attn_ln2_g"]),
        col8_ref[0](inputs["L1_attn_ln2_b"]),
    ], axis=1)

    return {
        "WvT": tr(inputs["L0_attn_value_w"]),
        "WrT": tr(inputs["L0_attn_recep_w"]),
        "WoT": tr(inputs["L0_attn_out_w"]),
        "gw": t(inputs["L0_ffn_gate_w"]),
        "w1": t(inputs["L0_ffn_w1"]),
        "w3": t(inputs["L0_ffn_w3"]),
        "w2": t(inputs["L0_ffn_w2"]),
        "Wr1": tr(inputs["L1_attn_Wr"]),
        "Ww1": tr(inputs["L1_attn_Ww"]),
        "Wk1": tr(inputs["L1_attn_Wk"]),
        "Wv1": tr(inputs["L1_attn_Wv"]),
        "Wo1": tr(inputs["L1_attn_Wo"]),
        "Wky": tr(inputs["L1_attn_Wkey"]),
        "Wvl": tr(inputs["L1_attn_Wval"]),
        "Wrc": tr(inputs["L1_attn_Wrec"]),
        "rw": tr(inputs["L1_ffn_router_w"]),
        "rb": t(inputs["L1_ffn_router_b"]).reshape(1, NE),
        "ew": np.ascontiguousarray(
            np.asarray(inputs["L1_ffn_expert_w"], dtype=f32).transpose(0, 2, 1)),
        "ebv": t(inputs["L1_ffn_expert_b"]),
        "smal": smal,
        "idn": np.eye(128, dtype=f32),
    }


def kernel(**inputs):
    x = np.asarray(inputs["L0_attn_x"], dtype=np.float32)
    shared = _prep_shared(inputs)
    nc = build_program()
    in_maps = []
    for b in range(8):
        m = dict(shared)
        m["x"] = np.ascontiguousarray(x[b])
        m["xT"] = np.ascontiguousarray(x[b].T)
        in_maps.append(m)
    res = run_bass_kernel_spmd(nc, in_maps, core_ids=list(range(8)))
    return np.stack([r["out"] for r in res.results], axis=0)
